# version 35
# speedup vs baseline: 1.1719x; 1.0001x over previous
import numpy as np
import ml_dtypes

import concourse.bass as bass
import concourse.mybir as mybir
from concourse.bass_utils import run_bass_kernel_spmd

F32 = mybir.dt.float32
BF16 = mybir.dt.bfloat16
AF = mybir.ActivationFunctionType
ALU = mybir.AluOpType
AX = mybir.AxisListType

D = 1024
FF = 4096
NPAIR = 8
TGA = 512
TGC = 256
EPS = 1e-5


class _Op:
    __slots__ = ("eng", "sem", "emit", "deps", "n_inc", "val", "tag", "is_dma")

    def __init__(self, eng, sem, emit, deps, n_inc, tag, is_dma):
        self.eng = eng
        self.sem = sem
        self.emit = emit
        self.deps = deps
        self.n_inc = n_inc
        self.val = None
        self.tag = tag
        self.is_dma = is_dma


class Sched:

    def __init__(self, nc):
        self.nc = nc
        self.ops = []
        self.sems = {}
        for name in ("pe", "act", "dve"):
            self._sem(name)
        self._barrier_tokens = None
        self._barrier_seen = None

    def _sem(self, name):
        if name not in self.sems:
            self.sems[name] = self.nc.alloc_semaphore("sem_" + name)
        return self.sems[name]

    def add(self, eng, emit, deps=(), tag="", n_dma=0, sem=None):
        sem = sem or eng
        self._sem(sem)
        deps = [d for d in deps if d is not None]
        if self._barrier_tokens is not None and eng not in self._barrier_seen:
            deps = list(self._barrier_tokens) + deps
            self._barrier_seen.add(eng)
            if len(self._barrier_seen) == 5:
                self._barrier_tokens = None
        is_dma = n_dma > 0
        op = _Op(eng, sem, emit, deps, 16 * n_dma if is_dma else 1, tag, is_dma)
        self.ops.append(op)
        return op

    def barrier(self):
        last = {}
        for op in self.ops:
            last[op.sem] = op
        self._barrier_tokens = list(last.values())
        self._barrier_seen = set()

    def finalize(self):
        cnt = {k: 0 for k in self.sems}
        for op in self.ops:
            cnt[op.sem] += op.n_inc
            op.val = cnt[op.sem]
        self.final = cnt

    def emit_engine(self, name, eng):
        sems = self.sems
        wm = {}
        for op in self.ops:
            if op.eng != name:
                continue
            for d in op.deps:
                if d.eng == "pe" and name == "pe":
                    continue
                v = d.val
                if wm.get(d.sem, 0) >= v:
                    continue
                eng.wait_ge(sems[d.sem], v)
                wm[d.sem] = v
            last = op.emit()
            if not op.is_dma:
                assert last is not None, f"op {op.tag} returned no instruction"
                last.then_inc(sems[op.sem], 1)
        if name == "pool":
            for sname, val in sorted(self.final.items()):
                if val > 0 and sname not in ("pe", "act", "dve"):
                    eng.wait_ge(sems[sname], val)


def build_nc(S=4096, debug_taps=False):
    NTG = S // TGA
    NKT = S // 128
    NTT = S // 128
    NTGC = S // TGC

    nc = bass.Bass("TRN2", target_bir_lowering=False, debug=False)

    x_d = nc.dram_tensor("x", [S, D], F32, kind="ExternalInput").ap()
    wqkv_d = nc.dram_tensor("wqkv", [3 * D, D], F32, kind="ExternalInput").ap()
    bqkv_d = nc.dram_tensor("bqkv", [3 * D], F32, kind="ExternalInput").ap()
    wp_d = nc.dram_tensor("wp", [D, D], F32, kind="ExternalInput").ap()
    bp_d = nc.dram_tensor("bp", [D], F32, kind="ExternalInput").ap()
    w1_d = nc.dram_tensor("w1", [D, FF], F32, kind="ExternalInput").ap()
    b1_d = nc.dram_tensor("b1", [FF], F32, kind="ExternalInput").ap()
    w2_d = nc.dram_tensor("w2", [FF, D], F32, kind="ExternalInput").ap()
    b2_d = nc.dram_tensor("b2", [D], F32, kind="ExternalInput").ap()
    out_d = nc.dram_tensor("out", [S, D], F32, kind="ExternalOutput").ap()

    skind = "ExternalOutput" if debug_taps else "Internal"
    attn_d = nc.dram_tensor("attn_s", [NPAIR, 128, S], BF16, kind=skind).ap()
    x2T_d = nc.dram_tensor("x2T_s", [8, 128, S], BF16, kind=skind).ap()
    x2_d = nc.dram_tensor("x2_s", [S, D], F32, kind=skind).ap()
    r_d = nc.dram_tensor("r_s", [2, 1024], F32, kind="Internal").ap()
    rtmp_d = nc.dram_tensor("rtmp_s", [2, 1024], F32, kind="Internal").ap()
    if debug_taps:
        qkv_tap = nc.dram_tensor("qkv_tap", [3, 128, S], BF16, kind="ExternalOutput").ap()
        y3_tap = nc.dram_tensor("y3_tap", [S, D], F32, kind="ExternalOutput").ap()

    idf_d = nc.inline_tensor(np.eye(128, dtype=np.float32), "idf")
    idb_d = nc.inline_tensor(
        np.eye(128, dtype=np.float32).astype(ml_dtypes.bfloat16), "idb"
    )
    mk = np.zeros((128, 2048), dtype=np.float32)
    jj = np.arange(512)[None, :]
    ii = np.arange(128)[:, None]
    for d_ in range(4):
        mk[:, 512 * d_ : 512 * d_ + 512] = (jj - ii - 128 * d_ >= 0).astype(np.float32)
    masks_d = nc.inline_tensor(mk.astype(ml_dtypes.bfloat16), "cmask")

    idf = nc.alloc_sbuf_tensor("idf_s", [128, 128], F32).ap()
    idb = nc.alloc_sbuf_tensor("idb_s", [128, 128], BF16).ap()
    masks = nc.alloc_sbuf_tensor("masks_s", [128, 2048], BF16).ap()
    ones_bf = nc.alloc_sbuf_tensor("ones_bf", [128, 512], BF16).ap()
    ones_f = nc.alloc_sbuf_tensor("ones_f", [128, 128], F32).ap()
    rowbf = nc.alloc_sbuf_tensor("rowbf", [128, 5632], BF16).ap()
    rowf = nc.alloc_sbuf_tensor("rowf", [128, 1024], F32).ap()
    rb = nc.alloc_sbuf_tensor("rb", [64, 2048], F32).ap()
    stats = nc.alloc_sbuf_tensor("stats", [128, 96], F32).ap()
    aq = nc.alloc_sbuf_tensor("aq", [128, 16], F32).ap()
    b1nat = nc.alloc_sbuf_tensor("b1nat", [32, 128], F32).ap()
    b1f = nc.alloc_sbuf_tensor("b1f", [128, 32], F32).ap()
    cbt = nc.alloc_sbuf_tensor("cbt", [128, 2], F32).ap()
    cvals = nc.alloc_sbuf_tensor("cvals", [128, 4], F32).ap()
    nc.const_aps.register(F32, EPS, cvals[:, 0:1])
    nc.const_aps.register(F32, 0.125, cvals[:, 1:2])
    nc.const_aps.register(F32, -0.5, cvals[:, 2:3])

    avail = nc.sbuf_bytes_remaining
    AR_COLS = (avail - 1024) // 2
    AR = nc.alloc_sbuf_tensor("arena", [128, AR_COLS], BF16).ap()

    def bf(off, n):
        assert off + n <= AR_COLS, f"arena overflow {off}+{n} > {AR_COLS}"
        return AR[:, off : off + n]

    def f32v(off, n):
        assert off % 2 == 0 and off + 2 * n <= AR_COLS, f"arena f32 ovf {off}"
        return AR[:, off : off + 2 * n].bitcast(F32)

    A_xT = 0
    A_q2 = A_xT + 8 * S
    A_k2 = A_q2 + 2 * S
    A_va = A_k2 + 2 * S
    A_wqT = A_va + 3 * 2 * 65 * NKT
    A_wvT = A_wqT + 2 * 2048
    A_pb = A_wvT + 2048
    A_attnC = A_pb + 2 * 1024
    A_xstage = A_attnC + S
    A_wcast = A_xstage + 4096
    A_sq = A_wcast + 3072
    A_u = A_sq + 1024
    A_END = A_u + 3 * 2048
    assert A_END <= AR_COLS, f"A map {A_END} > {AR_COLS}"

    B_w1 = 0
    B_attn = B_w1 + 8 * FF
    B_wpT = B_attn + 2 * 8 * TGA
    B_x2b = B_wpT + 8 * 1024
    B_x2T = B_x2b + 4096
    B_wc = B_x2T
    B_xtg = B_x2T + 8 * TGA
    B_y2 = B_xtg + 8192
    B_x2 = B_y2 + 2 * 8192
    B_END = B_x2 + 8192
    assert B_END <= AR_COLS, f"B map {B_END} > {AR_COLS}"

    C_w1 = 0
    C_w2 = C_w1 + 8 * FF
    C_h = C_w2 + 32 * 1024
    C_x2T = C_h + 4 * TGC
    C_x2 = C_x2T + 2 * 8 * TGC
    C_out = C_x2 + 2 * 4096
    C_scr = C_out + 2 * 4096
    C_END = C_scr + 2048
    assert C_END <= AR_COLS, f"C map {C_END} > {AR_COLS}"

    PS = nc.alloc_psum_tensor("ps", [128, 4096], F32).ap()

    def bank(i):
        return PS[:, 512 * i : 512 * i + 512]

    def bank_bf(i):
        return PS[:, 512 * i : 512 * i + 512].bitcast(BF16)

    sch = Sched(nc)
    t = nc.tensor
    v = nc.vector
    sc = nc.scalar
    sp = nc.sync
    gp = nc.gpsimd

    def sp_dma(stream, outs_ins, deps=(), tag=""):
        sem = sch._sem(stream)

        def em():
            for o, i in outs_ins:
                sp.dma_start(out=o, in_=i).then_inc(sem, 16)
        return sch.add("sp", em, deps, tag, n_dma=len(outs_ins), sem=stream)

    def gp_dma(stream, outs_ins, deps=(), tag=""):
        sem = sch._sem(stream)

        def em():
            for o, i in outs_ins:
                gp.dma_start(out=o, in_=i).then_inc(sem, 16)
        return sch.add("pool", em, deps, tag, n_dma=len(outs_ins), sem=stream)

    c0 = sp_dma(
        "spc",
        [
            (idf, idf_d.ap()),
            (idb, idb_d.ap()),
            (masks, masks_d.ap()),
            (b1nat, b1_d.rearrange("(c p) -> c p", p=128)),
        ],
        tag="consts",
    )
    m0a = sch.add("dve", lambda: v.memset(cvals[:, 0:1], EPS), (), "c_eps")
    m0b = sch.add("dve", lambda: v.memset(cvals[:, 1:2], 0.125), (m0a,), "c_s8")
    m0c = sch.add("dve", lambda: v.memset(cvals[:, 2:3], -0.5), (m0b,), "c_mh")
    m1 = sch.add("dve", lambda: v.memset(ones_bf, 1.0), (m0c,), "ones_bf")
    m2 = sch.add("dve", lambda: v.memset(ones_f, 1.0), (m1,), "ones_f")
    m3 = sch.add("dve", lambda: v.memset(rowbf[0:1, 3072:3584], 1.0), (m2,), "ones_row")
    va_both = bf(A_va, 3 * 2 * 65 * NKT)
    m4 = sch.add(
        "dve",
        lambda: v.memset(va_both.rearrange("p (a c) -> p a c", c=65)[:, :, 64:65], 1.0),
        (m3,),
        "va_ones",
    )
    cbias = gp_dma(
        "pbias",
        [
            (rowbf[0:1, 0:3072], bqkv_d[None, :]),
            (rowbf[0:1, 3584:4608], bp_d[None, :]),
            (rowbf[0:1, 4608:5632], b2_d[None, :]),
        ],
        tag="bias_cast",
    )
    tb1 = sch.add(
        "pe",
        lambda: t.transpose(PS[0:128, 2048:2080], b1nat, idf[0:32, 0:32]),
        (c0,),
        "b1T",
    )
    eb1 = sch.add("dve", lambda: v.tensor_copy(b1f, PS[0:128, 2048:2080]), (tb1,), "b1Tev")

    sch.barrier()

    xT = bf(A_xT, 8 * S)
    xT4 = xT.rearrange("p (f t c) -> p f t c", f=8, t=NTT, c=128)
    xstage_b = bf(A_xstage, 4096)
    last_ev = {0: None, 1: None}
    xT_ready = [None] * NTT

    xTb = xT
    wcastA = bf(A_wcast, 3072)

    prev_exp = [None, None]
    attnC_store = [None]
    attn_store_par = [None, None]
    prev_pair_uev = None
    prev_pair_nm = None
    pbuf_user = [None, None]
    bank45_user = {0: eb1, 1: eb1}
    str_prev = [None, None]
    ldw_prev = [None, None]
    rcw_prev = [None, None]
    prev_cbv = [None]
    nm_par = [None, None, None]
    nm_par2 = [None, None]
    qsq_user = [None, None]
    pending_vevs = {}
    handles = [None] * NPAIR
    wld_pending = {}
    vld_pending = {}
    vmm_defer = {}

    def make_start(p):
        pb = p % 2
        H_ = {
            "q2": bf(A_q2 + pb * S, S),
            "k2": bf(A_k2 + pb * S, S),
            "va": bf(A_va + (p % 3) * 2 * 65 * NKT, 2 * 65 * NKT),
            "wqT": bf(A_wqT + pb * 2048, 1024),
            "wkT": bf(A_wqT + pb * 2048 + 1024, 1024),
            "qk_ev": {},
            "qk_mma": {},
            "wev": [],
            "wtr": [],
            "vwtr": [],
            "cbcol": cbt[:, pb : pb + 1],
            "cbe": None,
        }
        if p % 2 == 0:
            H_["va_evs"] = []
            pending_vevs[p + 1] = []
        else:
            H_["va_evs"] = pending_vevs.pop(p)
        handles[p] = H_
        H_["va3"] = H_["va"].rearrange("p (h a c) -> p h a c", h=2, a=NKT, c=65)
        thunks = []
        nrm_toks = {}
        H_["wld"] = []

        def t_wld(wi):
            row0 = [128 * p, D + 128 * p][wi]
            stg = wcastA[:, 1024 * wi : 1024 * wi + 1024]
            ld = gp_dma(f"pw{wi}", [(stg, wqkv_d[row0 : row0 + 128, :])],
                        (), f"wld{p}.{wi}")
            H_["wld"].append(ld)

        def t_wld_next(wi):
            row0 = [128 * (p + 1), D + 128 * (p + 1)][wi]
            stg = wcastA[:, 1024 * wi : 1024 * wi + 1024]
            ld = gp_dma(f"pw{wi}", [(stg, wqkv_d[row0 : row0 + 128, :])],
                        (H_["wtr"][wi],), f"wld{p + 1}.{wi}")
            wld_pending[p + 1].append(ld)

        def t_wtr(wi):
            wdstT = [H_["wqT"], H_["wkT"]][wi]
            stg = wcastA[:, 1024 * wi : 1024 * wi + 1024]
            bki = 4 + wi % 2
            def em_wtr(stg=stg, bki=bki):
                r = None
                for k in range(8):
                    r = t.transpose(
                        bank_bf(bki)[:, 128 * k : 128 * k + 128],
                        stg[:, 128 * k : 128 * k + 128],
                        idb,
                    )
                return r
            tr = sch.add("pe", em_wtr, (H_["wld"][wi], bank45_user[wi % 2]), f"wT{p}.{wi}")
            def em_wev(wdstT=wdstT, bki=bki):
                return v.tensor_copy(wdstT, bank_bf(bki)[:, 0:1024])
            ev = sch.add("dve", em_wev, (tr,), f"wTev{p}.{wi}")
            H_["wev"].append(ev)
            H_["wtr"].append(tr)
            bank45_user[wi % 2] = ev

        if p == 0:
            for wi in range(2):
                thunks.append((-1, lambda wi=wi: t_wld(wi)))
        else:
            H_["wld"] = wld_pending.pop(p)
        for wi in range(2):
            thunks.append((0, lambda wi=wi: t_wtr(wi)))
        if p + 1 < NPAIR:
            wld_pending[p + 1] = []
            for wi in range(2):
                thunks.append((-1, lambda wi=wi: t_wld_next(wi)))

        def t_qk(qi, g):
            wT = [H_["wqT"], H_["wkT"]][qi]
            dst = [H_["q2"], H_["k2"]][qi]
            brow = [128 * p, D + 128 * p][qi]
            idx = qi * NTG + g
            bki = 4 + idx % 2
            def em_qk_a(wT=wT, g=g, bki=bki):
                r = None
                for fc in range(4):
                    r = t.matmul(
                        bank(bki),
                        wT[:, 128 * fc : 128 * fc + 128],
                        xTb[:, S * fc + TGA * g : S * fc + TGA * g + TGA],
                        start=(fc == 0),
                        stop=False,
                    )
                return r
            def em_qk_b(wT=wT, g=g, bki=bki, brow=brow):
                for fc in range(4, 8):
                    t.matmul(
                        bank(bki),
                        wT[:, 128 * fc : 128 * fc + 128],
                        xTb[:, S * fc + TGA * g : S * fc + TGA * g + TGA],
                        start=False,
                        stop=False,
                    )
                return t.matmul(
                    bank(bki),
                    rowbf[0:1, brow : brow + 128],
                    rowbf[0:1, 3072 : 3072 + TGA],
                    start=False,
                    stop=True,
                )
            mma = sch.add(
                "pe", em_qk_a,
                (H_["wev"][qi], bank45_user[idx % 2], m3, cbias,
                 xT_ready[min(4 * g + 3, NTT - 1)]),
                f"qka{p}.{qi}.{g}",
            )
            H_["qk_mma"][(qi, g)] = (mma, em_qk_b, dst, bki)

        def t_qk_b(qi, g):
            idx = qi * NTG + g
            (mma, em_qk_b, dst, bki) = H_["qk_mma"][(qi, g)]
            mm = sch.add("pe", em_qk_b, (mma,), f"qk{p}.{qi}.{g}")
            def em_ev(dst=dst, g=g, bki=bki):
                return v.tensor_copy(dst[:, TGA * g : TGA * g + TGA], bank(bki))
            ev = sch.add("dve", em_ev, (mm,), f"qkev{p}.{qi}.{g}")
            bank45_user[idx % 2] = ev
            H_["qk_ev"][(qi, g)] = ev

        def t_nrm(qi, g):
            srcb = [H_["q2"], H_["k2"]][qi]
            sl = g % 2
            qsq = bf(A_sq + sl * 512, 512)
            sqg = sch.add(
                "dve",
                lambda srcb=srcb, g=g, qsq=qsq: v.tensor_mul(
                    qsq, srcb[:, TGA * g : TGA * g + TGA], srcb[:, TGA * g : TGA * g + TGA]
                ),
                (H_["qk_ev"][(qi, g)], prev_cbv[0], qsq_user[sl]),
                f"nsq{p}.{qi}.{g}",
            )
            bki = 4 + g % 2
            mmn = sch.add(
                "pe",
                lambda qsq=qsq, bki=bki: t.matmul(
                    bank(bki)[0:1, :],
                    ones_bf[0:128, 0:1],
                    qsq,
                    start=True,
                    stop=True,
                ),
                (sqg, bank45_user[g % 2]),
                f"nmm{p}.{qi}.{g}",
            )
            qsq_user[sl] = mmn
            rd = sch.add(
                "dve",
                lambda qi=qi, g=g, bki=bki: v.reduce_max(
                    stats[0:1, 40 + 8 * qi + g : 41 + 8 * qi + g],
                    bank(bki)[0:1, :],
                    axis=AX.X,
                ),
                (mmn,),
                f"nrd{p}.{qi}.{g}",
            )
            bank45_user[g % 2] = rd
            nrm_toks[(qi, g)] = rd

        for g in range(NTG):
            for qi in range(2):
                thunks.append((4 * g + 3, lambda qi=qi, g=g: t_qk(qi, g)))
                thunks.append((4 * g + 3, lambda qi=qi, g=g: t_qk_b(qi, g)))
            for qi in range(2):
                thunks.append((4 * g + 3, lambda qi=qi, g=g: t_nrm(qi, g)))

        def t_cb():
            qn2 = sch.add(
                "dve",
                lambda: v.reduce_max(stats[0:1, 38:39], stats[0:1, 40 : 40 + NTG], axis=AX.X),
                tuple(nrm_toks[(0, g)] for g in range(NTG)),
                f"qn2{p}",
            )
            kn2 = sch.add(
                "dve",
                lambda: v.reduce_max(stats[0:1, 39:40], stats[0:1, 48 : 48 + NTG], axis=AX.X),
                tuple(nrm_toks[(1, g)] for g in range(NTG)),
                f"kn2{p}",
            )
            cb0 = sch.add(
                "dve",
                lambda: v.scalar_tensor_tensor(
                    stats[0:1, 37:38], stats[0:1, 38:39], 1.0, stats[0:1, 39:40],
                    ALU.mult, ALU.add,
                ),
                (qn2, kn2),
                f"cb0{p}",
            )
            cbv = sch.add(
                "dve",
                lambda: v.tensor_scalar_mul(stats[0:1, 36:37], stats[0:1, 37:38], -0.0625),
                (cb0,),
                f"cbv{p}",
            )
            prev_cbv[0] = cbv
            cbm = sch.add(
                "pe",
                lambda: t.matmul(
                    PS[0:128, 2176:2177], ones_f[0:1, 0:128], stats[0:1, 36:37],
                    start=True, stop=True,
                ),
                (cbv, bank45_user[0]),
                f"cbm{p}",
            )
            cbe = sch.add(
                "dve",
                lambda: v.tensor_copy(H_["cbcol"], PS[0:128, 2176:2177]),
                (cbm, prev_exp[0], prev_exp[1]),
                f"cbe{p}",
            )
            H_["cbe"] = cbe
            bank45_user[0] = cbe

        thunks.append((NKT - 1, t_cb))

        if p % 2 == 1 and p + 1 < NPAIR:
            vld_pending[p + 1] = []

            def t_vld_next(r):
                row0 = 2 * D + 128 * (p + 1) + 128 * r
                stg = xstage_b[:, 1024 * r : 1024 * r + 1024]
                if p + 1 == 2:
                    dep = (handles[0]["vwtr"][1], xT_ready[NTT - 1])
                else:
                    dep = (handles[p - 1]["vwtr"][r],)
                ld = gp_dma("pw2" if r == 0 else "pw3",
                            [(stg, wqkv_d[row0 : row0 + 128, :])],
                            dep, f"vwld{p + 1}.{r}")
                vld_pending[p + 1].append(ld)

            for r_ in (1, 0):
                thunks.insert(0, (-1, lambda r=r_: t_vld_next(r)))
        if p % 2 == 1:
            for i_, th_ in enumerate(vmm_defer.pop(p, [])):
                thunks.insert(6 + i_, th_)

        if p % 2 == 0:
            wvg = bf(A_wvT, 2048)
            va_next = bf(A_va + ((p + 1) % 3) * 2 * 65 * NKT, 2 * 65 * NKT)
            va3_next = va_next.rearrange("p (h a c) -> p h a c", h=2, a=NKT, c=65)
            vevs_next = pending_vevs[p + 1]
            vstate = {"wld": [] if p == 0 else vld_pending.pop(p)}

            def t_vld(r):
                row0 = 2 * D + 128 * p + 128 * r
                stg = wcastA[:, 2048:3072]
                dep_stage = H_["vwtr"][r - 1] if r >= 1 else None
                ld = gp_dma("pw2", [(stg, wqkv_d[row0 : row0 + 128, :])],
                            (dep_stage,), f"vwld{p}.{r}")
                vstate["wld"].append(ld)

            def t_vtr(r):
                if p == 0:
                    stg = wcastA[:, 2048:3072]
                else:
                    stg = xstage_b[:, 1024 * r : 1024 * r + 1024]
                bki = 4 + r % 2
                def em_wtr(stg=stg, bki=bki):
                    rr = None
                    for k in range(8):
                        rr = t.transpose(
                            bank_bf(bki)[:, 128 * k : 128 * k + 128],
                            stg[:, 128 * k : 128 * k + 128],
                            idb,
                        )
                    return rr
                tr = sch.add("pe", em_wtr, (vstate["wld"][r], bank45_user[r % 2]), f"vwT{p}.{r}")
                def em_wev(r=r, bki=bki):
                    return v.tensor_copy(
                        wvg.rearrange("p (f pr c) -> p f pr c", f=8, pr=2, c=128)[:, :, r, :],
                        bank_bf(bki).rearrange("p (a c) -> p a c", a=8, c=128),
                    )
                ev = sch.add("dve", em_wev, (tr,), f"vwTev{p}.{r}")
                H_["vwtr"].append(ev)
                bank45_user[r % 2] = ev

            def mk_vg(a):
                par = a % 2
                bki = 4 + par
                def em_v_a(a=a, bki=bki):
                    r = None
                    for fc in range(4):
                        r = t.matmul(
                            bank(bki)[:, 0:256],
                            xTb[:, S * fc + 128 * a : S * fc + 128 * a + 128],
                            wvg[:, 256 * fc : 256 * fc + 256],
                            start=(fc == 0),
                            stop=False,
                        )
                    return r
                def em_v_b(a=a, bki=bki):
                    for fc in range(4, 8):
                        t.matmul(
                            bank(bki)[:, 0:256],
                            xTb[:, S * fc + 128 * a : S * fc + 128 * a + 128],
                            wvg[:, 256 * fc : 256 * fc + 256],
                            start=False,
                            stop=False,
                        )
                    return t.matmul(
                        bank(bki)[:, 0:256],
                        rowbf[0:1, 3072 : 3072 + 128],
                        rowbf[0:1, 2 * D + 128 * p : 2 * D + 128 * p + 256],
                        start=False,
                        stop=True,
                    )
                mma = sch.add(
                    "pe", em_v_a,
                    (H_["vwtr"][1], bank45_user[par], m4, cbias,
                     xT_ready[min(a, NTT - 1)]),
                    f"va{p}.{a}",
                )
                vstate[("mma", a)] = (mma, em_v_b)

            def mk_vg_b(a):
                par = a % 2
                bki = 4 + par
                (mma, em_v_b) = vstate[("mma", a)]
                mm = sch.add("pe", em_v_b, (mma,), f"v{p}.{a}")
                def em_vev(a=a, bki=bki):
                    return v.tensor_copy(
                        H_["va3"][:, :, a, 0:64],
                        bank(bki)[:, 0:128].rearrange("p (h c) -> p h c", h=2, c=64),
                    )
                ev = sch.add("dve", em_vev, (mm,), f"vev{p}.{a}")
                H_["va_evs"].append(ev)
                def em_vev2(a=a, bki=bki):
                    return v.tensor_copy(
                        va3_next[:, :, a, 0:64],
                        bank(bki)[:, 128:256].rearrange("p (h c) -> p h c", h=2, c=64),
                    )
                ev2 = sch.add("dve", em_vev2, (ev, mm), f"vev2{p}.{a}")
                bank45_user[par] = ev2
                vevs_next.append(ev2)

            if p == 0:
                thunks.append((-1, lambda: t_vld(0)))
                thunks.append((0, lambda: t_vtr(0)))
                thunks.append((-1, lambda: t_vld(1)))
                thunks.append((1, lambda: t_vtr(1)))
            else:
                thunks.append((0, lambda: t_vtr(0)))
                thunks.append((1, lambda: t_vtr(1)))
            a_defer = NKT
            for a in range(a_defer):
                thunks.append((a, lambda a=a: mk_vg(a)))
                thunks.append((a, lambda a=a: mk_vg_b(a)))
            dl = []
            for a in range(a_defer, NKT):
                dl.append((a, lambda a=a: mk_vg(a)))
                dl.append((a, lambda a=a: mk_vg_b(a)))
            vmm_defer[p + 1] = dl
        return thunks

    thunks0 = make_start(0)
    t0i = 0
    for tp_ in range(NTT // 2):
        b_ = tp_ % 2
        stg = xstage_b[:, 2048 * b_ : 2048 * b_ + 2048]
        ld = gp_dma(
            f"px{b_}",
            [(
                stg.rearrange("p (c f) -> p c f", c=2, f=1024),
                x_d[256 * tp_ : 256 * tp_ + 256, :].rearrange("(c p) f -> p c f", p=128),
            )],
            (last_ev[b_],),
            f"xld{tp_}",
        )
        def em_tr(stg=stg, b_=b_):
            r = None
            for k in range(16):
                r = t.transpose(
                    PS[:, 1024 * b_ : 1024 * b_ + 1024].bitcast(BF16)[:, 128 * k : 128 * k + 128],
                    stg[:, 128 * k : 128 * k + 128],
                    idb,
                )
            return r
        tr = sch.add("pe", em_tr, (ld, last_ev[b_]), f"xT{tp_}")
        def em_ev(tp_=tp_, b_=b_):
            return v.tensor_copy(
                xT4[:, :, 2 * tp_ : 2 * tp_ + 2, :].transpose([0, 2, 1, 3]),
                PS[:, 1024 * b_ : 1024 * b_ + 1024]
                .bitcast(BF16)
                .rearrange("p (c a e) -> p c a e", c=2, a=8, e=128),
            )
        ev = sch.add("dve", em_ev, (tr,), f"xTev{tp_}")
        last_ev[b_] = ev
        xT_ready[2 * tp_] = ev
        xT_ready[2 * tp_ + 1] = ev
        while t0i < len(thunks0) and thunks0[t0i][0] <= 2 * tp_ + 1:
            thunks0[t0i][1]()
            t0i += 1
    while t0i < len(thunks0):
        thunks0[t0i][1]()
        t0i += 1

    preB_wp = []
    preB_w1 = []
    preB_slots = []
    for p in range(NPAIR):
        H_ = handles[p]
        q2 = H_["q2"]
        k2 = H_["k2"]
        va = H_["va"]
        qk_ev = H_["qk_ev"]
        va_evs = H_["va_evs"]
        cbe = H_["cbe"]
        cbcol = H_["cbcol"]
        attnC = bf(A_attnC, S)
        nxt = make_start(p + 1) if p + 1 < NPAIR else []
        nxt_i = 0
        if p == NPAIR - 1 and S >= FF:
            wp_slots = [
                (wcastA[:, 0:1024], H_["wtr"][0]),
                (wcastA[:, 1024:2048], H_["wtr"][1]),
                (xstage_b[:, 0:1024], handles[6]["vwtr"][0]),
                (xstage_b[:, 1024:2048], handles[6]["vwtr"][1]),
                (xstage_b[:, 2048:3072], xT_ready[NTT - 1]),
                (xstage_b[:, 3072:4096], xT_ready[NTT - 1]),
                (wcastA[:, 2048:3072], handles[0]["vwtr"][1]),
                (bf(A_sq, 1024), H_["cbe"]),
            ]
            for oc in range(8):
                stg, slot_dep = wp_slots[oc]
                chain = preB_wp[oc - 2] if oc >= 2 else None
                preB_wp.append(gp_dma(
                    f"pw{oc % 2}",
                    [(stg, wp_d[128 * oc : 128 * oc + 128, :])],
                    (slot_dep, chain),
                    f"wpld{oc}",
                ))
            preB_slots.extend(s for s, _ in wp_slots)
            if True:
                lastqk = [H_["qk_ev"][(qi, NTG - 1)] for qi in range(2)]
                for fc in range(8):
                    dep = (preB_w1[fc - 2],) if fc >= 2 else tuple(lastqk)
                    preB_w1.append(gp_dma(
                        f"pq{fc % 2}",
                        [(bf(0, 8 * FF)[:, FF * fc : FF * fc + FF],
                          w1_d[128 * fc : 128 * fc + 128, :])],
                        dep,
                        f"w1ld{fc}",
                    ))

        uev_prev = None
        norm_ops = []
        pending_tail = []
        tail_due = []
        units = [(g, j) for g in range(NTG) for j in range(4 * (g + 1))]

        def tail_stage1(gt, attnU, uev):
            g2_ = gt % 2
            dwide = rowf[:, 32 * g2_ : 32 * g2_ + 8]
            st_d = sp_dma(
                f"sr{g2_}",
                [(rtmp_d[g2_ : g2_ + 1, :], attnU[64:65, 0:1024])],
                (uev, ldw_prev[g2_], nm_par2[g2_]),
                f"std{p}.{gt}",
            )
            ld_w = sp_dma(
                f"sw{g2_}",
                [(dwide, rtmp_d[g2_ : g2_ + 1, :].rearrange("o (q c) -> (o q) c", q=128))],
                (st_d, rcw_prev[g2_]),
                f"ldw{p}.{gt}",
            )
            ldw_prev[g2_] = ld_w
            return ld_w

        def flush_tail():
            (gt, attnU, uev, ld_w) = pending_tail.pop(0)
            g3 = gt % 3
            g2_ = gt % 2
            dwide = rowf[:, 32 * g2_ : 32 * g2_ + 8]
            rwide = rowf[:, 32 * g2_ + 8 : 32 * g2_ + 16]
            rc = sch.add(
                "dve",
                lambda dwide=dwide, rwide=rwide: v.reciprocal(rwide, dwide),
                (ld_w, str_prev[g2_]),
                f"rc{p}.{gt}",
            )
            rcw_prev[g2_] = rc
            st_r = sp_dma(
                f"sr{g2_}",
                [(r_d[g2_ : g2_ + 1, :].rearrange("o (q c) -> (o q) c", q=128), rwide)],
                (rc, nm_par2[g2_]),
                f"str{p}.{gt}",
            )
            str_prev[g2_] = st_r
            ld_rb = gp_dma(
                f"pl{g2_}",
                [(
                    rb[0:64, 1024 * g2_ : 1024 * g2_ + 1024],
                    r_d[g2_ : g2_ + 1, :].broadcast_to([64, 1024]),
                )],
                (st_r, nm_par2[g2_]),
                f"ldrb{p}.{gt}",
            )
            rbase = 1024 * g2_
            nm = sch.add(
                "pool",
                lambda gt=gt, attnU=attnU, attnC=attnC, rbase=rbase: (
                    gp.tensor_tensor(
                        attnC[0:64, TGA * gt : TGA * gt + TGA],
                        attnU[0:64, 0:512],
                        rb[0:64, rbase : rbase + 512],
                        ALU.mult,
                    ),
                    gp.tensor_tensor(
                        attnC[64:128, TGA * gt : TGA * gt + TGA],
                        attnU[0:64, 512:1024],
                        rb[0:64, rbase + 512 : rbase + 1024],
                        ALU.mult,
                    ),
                )[-1],
                (ld_rb, attnC_store[0] if len(norm_ops) == 0 else None),
                f"nm{p}.{gt}",
            )
            nm_par[g3] = nm
            nm_par2[g2_] = nm
            norm_ops.append(nm)

        def mk_st(ui):
            g, j = units[ui]
            ub = ui % 2
            def em_st(g=g, j=j, ub=ub, q2=q2, k2=k2):
                r = None
                for hh in range(2):
                    r = t.matmul(
                        PS[:, 1024 * ub + 512 * hh : 1024 * ub + 512 * hh + 512],
                        k2[64 * hh : 64 * hh + 64, 128 * j : 128 * j + 128],
                        q2[64 * hh : 64 * hh + 64, TGA * g : TGA * g + TGA],
                        start=True,
                        stop=True,
                    )
                return r
            return sch.add(
                "pe",
                em_st,
                (prev_exp[ub], qk_ev[(0, g)], qk_ev[(1, g)]),
                f"st{p}.{g}.{j}",
            )

        st_next = mk_st(0)
        for ui, (g, j) in enumerate(units):
            while pending_tail and g >= pending_tail[0][0] + 2:
                flush_tail()
            nunit = 4 * (g + 1)
            ub = ui % 2
            pb3 = ui % 2
            Pb = bf(A_pb + pb3 * 1024, 1024)
            quad = PS[:, 1024 * ub : 1024 * ub + 1024]
            st = st_next
            diag = j >= nunit - 4
            mi = j - (nunit - 4) if diag else 0
            dcol = 128 * mi
            if not diag:
                def em_exp(Pb=Pb, quad=quad, cbcol=cbcol):
                    return sc.activation(Pb, quad, AF.Exp, bias=cbcol, scale=0.125)
                ex = sch.add("act", em_exp, (st, cbe, pbuf_user[pb3]), f"exp{p}.{g}.{j}")
                prev_exp[ub] = ex
                pv_dep = ex
            else:
                ms0 = None
                if dcol > 0:
                    ms0 = sch.add(
                        "dve",
                        lambda Pb=Pb, dcol=dcol: v.memset(
                            Pb.rearrange("p (h c) -> p h c", h=2, c=512)[:, :, 0:dcol], 0.0
                        ),
                        (pbuf_user[pb3],),
                        f"pz{p}.{g}.{j}",
                    )
                def em_exp(Pb=Pb, quad=quad, cbcol=cbcol, dcol=dcol):
                    return sc.activation(
                        Pb.rearrange("p (h c) -> p h c", h=2, c=512)[:, :, dcol:],
                        quad.rearrange("p (h c) -> p h c", h=2, c=512)[:, :, dcol:],
                        AF.Exp, bias=cbcol, scale=0.125,
                    )
                ex = sch.add("act", em_exp, (st, cbe, pbuf_user[pb3]), f"exp{p}.{g}.{j}")
                prev_exp[ub] = ex
                def em_mask(Pb=Pb, dcol=dcol):
                    r = None
                    for hh in range(2):
                        r = v.tensor_tensor(
                            Pb[:, 512 * hh + dcol : 512 * hh + dcol + 128],
                            Pb[:, 512 * hh + dcol : 512 * hh + dcol + 128],
                            masks[:, 0:128],
                            ALU.mult,
                        )
                    return r
                msk = sch.add("dve", em_mask, (ex, ms0), f"msk{p}.{g}.{j}")
                pv_dep = msk
            st_next = mk_st(ui + 1) if ui + 1 < len(units) else None
            while nxt_i < len(nxt) and nxt[nxt_i][0] < 0:
                nxt[nxt_i][1]()
                nxt_i += 1
            if nxt_i < len(nxt) and ui >= 8:
                nxt[nxt_i][1]()
                nxt_i += 1
            def em_pv(j=j, Pb=Pb, va=va, first=(j == 0), last=(j == nunit - 1)):
                r = None
                for hh in range(2):
                    r = t.matmul(
                        PS[0:65, 3072 + 512 * hh : 3584 + 512 * hh],
                        va[:, 65 * (NKT * hh + j) : 65 * (NKT * hh + j) + 65],
                        Pb[:, 512 * hh : 512 * hh + 512],
                        start=first,
                        stop=last,
                    )
                return r
            deps_pv = [pv_dep, va_evs[j]]
            if j == 0:
                deps_pv.append(uev_prev if g > 0 else prev_pair_uev)
            pv = sch.add("pe", em_pv, tuple(deps_pv), f"pv{p}.{g}.{j}")
            pbuf_user[pb3] = pv
            if j != nunit - 1:
                continue
            g3 = g % 3
            attnU = f32v(A_u + g3 * 2048, 1024)
            uev = sch.add(
                "dve",
                lambda attnU=attnU: v.tensor_copy(attnU[0:65, :], PS[0:65, 3072:4096]),
                (pv, nm_par[g3]),
                f"uev{p}.{g}",
            )
            ld_w = tail_stage1(g, attnU, uev)
            pending_tail.append((g, attnU, uev, ld_w))
            uev_prev = uev
        while pending_tail:
            flush_tail()
        while nxt_i < len(nxt):
            nxt[nxt_i][1]()
            nxt_i += 1
        st_op = gp_dma(f"pa{p % 2}", [(attn_d[p], attnC)], tuple(norm_ops), f"attn_st{p}")
        attnC_store[0] = st_op
        attn_store_par[p % 2] = st_op
        prev_pair_uev = uev_prev
        prev_pair_nm = norm_ops[-1]

        if debug_taps and p == 0:
            gp_dma(
                "pbias",
                [
                    (qkv_tap[0], q2),
                    (qkv_tap[1], k2),
                    (qkv_tap[2][:, 0 : 65 * NKT], va[:, 0 : 65 * NKT]),
                ],
                tuple(qk_ev.values()) + tuple(va_evs),
                "qkv_tap",
            )
            sch.barrier()

    lastA = {}
    for _op in sch.ops:
        lastA[_op.eng] = _op


    w1sb = bf(B_w1, 8 * FF)
    wpT = bf(B_wpT, 8 * 1024)
    x2b = bf(B_x2b, 4096)
    sq_scrB = f32v(B_x2b, 2048)
    x2Ttg = bf(B_x2T, 8 * TGA)
    x2T4 = x2Ttg.rearrange("p (f t c) -> p f t c", f=8, t=TGA // 128, c=128)
    wcB = bf(B_wc, 2048)
    x2f = f32v(B_x2, 4096)

    wp_evs = []
    wp_trs = []
    for oc in range(8):
        if preB_wp:
            stg = preB_slots[oc]
            ld = preB_wp[oc]
        elif oc < 2:
            stg = wcB[:, 1024 * (oc % 2) : 1024 * (oc % 2) + 1024]
            ld = gp_dma(f"pw{oc % 2}", [(stg, wp_d[128 * oc : 128 * oc + 128, :])],
                        (lastA["pe"], lastA["dve"], lastA["act"], lastA["pool"]), f"wpld{oc}")
        else:
            stg = wcB[:, 1024 * (oc % 2) : 1024 * (oc % 2) + 1024]
            ld = gp_dma(f"pw{oc % 2}", [(stg, wp_d[128 * oc : 128 * oc + 128, :])], (wp_trs[oc - 2],), f"wpld{oc}")
        def em_tr(stg=stg, oc=oc):
            r = None
            for k in range(8):
                r = t.transpose(
                    bank_bf(2 + oc % 2)[:, 128 * k : 128 * k + 128],
                    stg[:, 128 * k : 128 * k + 128],
                    idb,
                )
            return r
        if oc < 2:
            tr = sch.add("pe", em_tr, (ld, prev_exp[0], prev_exp[1]), f"wpT{oc}")
        else:
            tr = sch.add("pe", em_tr, (ld, wp_evs[oc - 2]), f"wpT{oc}")
        def em_ev(oc=oc):
            return v.tensor_copy(
                wpT.rearrange("p (f o c) -> p f o c", f=8, o=8, c=128)[:, :, oc, :],
                bank_bf(2 + oc % 2).rearrange("p (a c) -> p a c", a=8, c=128),
            )
        ev = sch.add("dve", em_ev, (tr,) + ((lastA["dve"],) if oc < 2 else ()), f"wpTev{oc}")
        wp_evs.append(ev)
        wp_trs.append(tr)

    if preB_w1:
        w1_lds = preB_w1
    else:
        w1_lds = []
        for fc in range(8):
            ld = gp_dma(
                f"pq{fc % 2}",
                [(w1sb[:, FF * fc : FF * fc + FF], w1_d[128 * fc : 128 * fc + 128, :])],
                (w1_lds[fc - 2] if fc >= 2 else None,),
                f"w1ld{fc}",
            )
            w1_lds.append(ld)

    attn_user = [None, None]
    xld_user = [None]
    proj_ev_last = {0: None, 1: None}
    x2t_ev_last = {0: None, 1: None}
    b_pending = [None]
    b_st1 = [None]
    b_x2t_trs = [[]]
    prev_st2 = None
    y2_readers_par = [(), ()]
    prev_sq = None
    for g in range(NTG):
        sb_ = 48 * (g % 2)
        atg = bf(B_attn + (g % 2) * 8 * TGA, 8 * TGA)
        xtg = f32v(B_xtg, 4096)
        y2 = f32v(B_y2 + (g % 2) * 8192, 4096)
        joinAB = (
            (lastA["pe"], lastA["dve"], attn_store_par[0], attn_store_par[1])
            if g < 2 else ()
        )
        a_ld = sp_dma(
            f"sa{g % 2}",
            [
                (atg[:, TGA * fc : TGA * fc + TGA], attn_d[fc][:, TGA * g : TGA * g + TGA])
                for fc in range(8)
            ],
            (attn_user[g % 2],) + joinAB,
            f"ald{g}",
        )
        x_ld = sp_dma(
            "sx0",
            [(
                xtg.rearrange("p (c f) -> p c f", c=4, f=1024),
                x_d[TGA * g : TGA * g + TGA, :].rearrange("(c p) f -> p c f", p=128),
            )],
            (xld_user[0],) + joinAB,
            f"xldB{g}",
        )
        y2_evs = []
        for tc in range(4):
            for oh in range(2):
                bki = (2 * tc + oh) % 2
                def em_proj(tc=tc, oh=oh, bki=bki, atg=atg):
                    r = None
                    for fc in range(8):
                        r = t.matmul(
                            bank(bki),
                            atg[:, TGA * fc + 128 * tc : TGA * fc + 128 * tc + 128],
                            wpT[:, 1024 * fc + 512 * oh : 1024 * fc + 512 * oh + 512],
                            start=(fc == 0),
                            stop=False,
                        )
                    return t.matmul(
                        bank(bki),
                        rowbf[0:1, 3072 : 3072 + 128],
                        rowbf[0:1, 3584 + 512 * oh : 3584 + 512 * oh + 512],
                        start=False,
                        stop=True,
                    )
                mm = sch.add(
                    "pe",
                    em_proj,
                    (a_ld, wp_evs[7], proj_ev_last[bki]),
                    f"proj{g}.{tc}.{oh}",
                )
                def em_y2(tc=tc, oh=oh, bki=bki, xtg=xtg, sb_=sb_, y2=y2):
                    return v.scalar_tensor_tensor(
                        y2[:, 1024 * tc + 512 * oh : 1024 * tc + 512 * oh + 512],
                        bank(bki),
                        1.0,
                        xtg[:, 1024 * tc + 512 * oh : 1024 * tc + 512 * oh + 512],
                        ALU.mult,
                        ALU.add,
                        accum_out=stats[:, sb_ + 2 * tc + oh : sb_ + 2 * tc + oh + 1],
                    )
                ev = sch.add(
                    "dve", em_y2,
                    (mm, x_ld) + y2_readers_par[g % 2]
                    + ((lastA["dve"], attnC_store[0]) if g < 2 else ()),
                    f"y2{g}.{tc}.{oh}",
                )
                proj_ev_last[bki] = ev
                y2_evs.append(ev)
        attn_user[g % 2] = y2_evs[-1]
        xld_user[0] = y2_evs[-1]
        if b_pending[0] is not None:
            b_pending[0]()
            b_pending[0] = None
        stat_ops = []
        sq_ops = []
        for tc in range(4):
            sq = sch.add(
                "act",
                lambda tc=tc, sb_=sb_, y2=y2: sc.activation(
                    sq_scrB[:, 0:1024],
                    y2[:, 1024 * tc : 1024 * tc + 1024],
                    AF.Square,
                    accum_out=stats[:, sb_ + 12 + tc : sb_ + 13 + tc],
                ),
                (y2_evs[2 * tc], y2_evs[2 * tc + 1], prev_sq) + tuple(b_x2t_trs[0])
                + ((lastA["act"],) if g < 2 else ()),
                f"sq{g}.{tc}",
            )
            prev_sq = sq
            sq_ops.append(sq)
            s_ = sch.add(
                "dve",
                lambda tc=tc, sb_=sb_: v.scalar_tensor_tensor(
                    stats[:, sb_ + 8 + tc : sb_ + 9 + tc],
                    stats[:, sb_ + 2 * tc : sb_ + 2 * tc + 1],
                    1.0,
                    stats[:, sb_ + 2 * tc + 1 : sb_ + 2 * tc + 2],
                    ALU.mult,
                    ALU.add,
                ),
                (y2_evs[2 * tc], y2_evs[2 * tc + 1]),
                f"s{g}.{tc}",
            )
            m_ = sch.add(
                "dve",
                lambda tc=tc, sb_=sb_: v.tensor_scalar_mul(
                    stats[:, sb_ + 16 + tc : sb_ + 17 + tc],
                    stats[:, sb_ + 8 + tc : sb_ + 9 + tc],
                    1.0 / D,
                ),
                (s_,),
                f"m{g}.{tc}",
            )
            mm2_ = sch.add(
                "dve",
                lambda tc=tc, sb_=sb_: v.scalar_tensor_tensor(
                    stats[:, sb_ + 20 + tc : sb_ + 21 + tc],
                    stats[:, sb_ + 16 + tc : sb_ + 17 + tc],
                    -1.0,
                    stats[:, sb_ + 16 + tc : sb_ + 17 + tc],
                    ALU.mult,
                    ALU.mult,
                ),
                (m_,),
                f"mm{g}.{tc}",
            )
            var2 = sch.add(
                "dve",
                lambda tc=tc, sb_=sb_: v.scalar_tensor_tensor(
                    stats[:, sb_ + 24 + tc : sb_ + 25 + tc],
                    stats[:, sb_ + 12 + tc : sb_ + 13 + tc],
                    1.0 / D,
                    stats[:, sb_ + 20 + tc : sb_ + 21 + tc],
                    ALU.mult,
                    ALU.add,
                ),
                (mm2_, sq),
                f"var{g}.{tc}",
            )
            stat_ops.append(var2)
        ln_ = sch.add(
            "act",
            lambda sb_=sb_: sc.activation(
                stats[:, sb_ + 28 : sb_ + 32], stats[:, sb_ + 24 : sb_ + 28],
                AF.Ln, bias=EPS,
            ),
            tuple(stat_ops),
            f"ln{g}",
        )
        rstd_ = sch.add(
            "act",
            lambda sb_=sb_: sc.activation(
                stats[:, sb_ + 32 : sb_ + 36], stats[:, sb_ + 28 : sb_ + 32],
                AF.Exp, scale=-0.5,
            ),
            (ln_,),
            f"rstd{g}",
        )
        nmr_ = sch.add(
            "dve",
            lambda sb_=sb_: v.scalar_tensor_tensor(
                stats[:, sb_ + 36 : sb_ + 40], stats[:, sb_ + 16 : sb_ + 20],
                -1.0, stats[:, sb_ + 32 : sb_ + 36],
                ALU.mult, ALU.mult,
            ),
            (rstd_,),
            f"nmr{g}",
        )
        x2_ops = []
        x2b_ops = []
        for tc in range(4):
            ap1 = sch.add(
                "act",
                lambda tc=tc, sb_=sb_, y2=y2: sc.activation(
                    x2f[:, 1024 * tc : 1024 * tc + 1024],
                    y2[:, 1024 * tc : 1024 * tc + 1024],
                    AF.Identity,
                    bias=stats[:, sb_ + 36 + tc : sb_ + 37 + tc],
                    scale=stats[:, sb_ + 32 + tc : sb_ + 33 + tc],
                ),
                (nmr_, prev_st2, sq_ops[-1]),
                f"ap1{g}.{tc}",
            )
            ap2 = sch.add(
                "act",
                lambda tc=tc, sb_=sb_, y2=y2: sc.activation(
                    x2b[:, 1024 * tc : 1024 * tc + 1024],
                    y2[:, 1024 * tc : 1024 * tc + 1024],
                    AF.Identity,
                    bias=stats[:, sb_ + 36 + tc : sb_ + 37 + tc],
                    scale=stats[:, sb_ + 32 + tc : sb_ + 33 + tc],
                ),
                (nmr_, sq_ops[-1]) + tuple(b_x2t_trs[0]),
                f"ap2{g}.{tc}",
            )
            x2_ops.append(ap1)
            x2b_ops.append(ap2)
        y2_readers_par[g % 2] = tuple(sq_ops + x2_ops + x2b_ops)
        st2 = gp_dma(
            "ps2",
            [(
                x2_d[TGA * g : TGA * g + TGA, :].rearrange("(c p) f -> p c f", p=128),
                x2f.rearrange("p (c f) -> p c f", c=4, f=1024),
            )],
            tuple(x2_ops),
            f"x2st{g}",
        )
        prev_st2 = st2

        def fin_b(g=g, x2b_ops=x2b_ops):
            trs = []
            evs = []
            for tc in range(4):
                bki = 2 + tc % 2
                def em_tr(tc=tc, bki=bki):
                    r = None
                    for fc in range(8):
                        r = t.transpose(
                            bank_bf(bki)[:, 128 * fc : 128 * fc + 128],
                            x2b[:, 1024 * tc + 128 * fc : 1024 * tc + 128 * fc + 128],
                            idb,
                        )
                    return r
                dep_bank = x2t_ev_last[tc % 2]
                if dep_bank is None:
                    dep_bank = wp_evs[-1]
                tr = sch.add("pe", em_tr, (x2b_ops[tc], dep_bank), f"x2t{g}.{tc}")
                def em_ev(tc=tc, bki=bki):
                    return v.tensor_copy(
                        x2T4[:, :, tc, :],
                        bank_bf(bki).rearrange("p (a c) -> p a c", a=8, c=128),
                    )
                ev = sch.add("dve", em_ev, (tr, b_st1[0]), f"x2tev{g}.{tc}")
                x2t_ev_last[tc % 2] = ev
                trs.append(tr)
                evs.append(ev)
            b_x2t_trs[0] = trs
            st1 = gp_dma(
                "ps1",
                [
                    (x2T_d[fc][:, TGA * g : TGA * g + TGA], x2Ttg[:, TGA * fc : TGA * fc + TGA])
                    for fc in range(8)
                ],
                tuple(evs),
                f"x2Tst{g}",
            )
            b_st1[0] = st1

        b_pending[0] = fin_b

    if b_pending[0] is not None:
        b_pending[0]()
        b_pending[0] = None

    sch.barrier()

    w2sb = bf(C_w2, 32 * 1024)
    sq_scrC = f32v(C_scr, 1024)
    w2_lds = []
    for hc in range(32):
        ld = gp_dma(
            f"pw{hc % 2}",
            [(w2sb[:, 1024 * hc : 1024 * hc + 1024], w2_d[128 * hc : 128 * hc + 128, :])],
            (w2_lds[hc - 2] if hc >= 2 else None,),
            f"w2ld{hc}",
        )
        w2_lds.append(ld)

    F1BANKS = [0, 1, 6, 7]
    h_user = [None, None, None, None]
    gelu_bank = [None, None, None, None]
    f1ops = {}
    x2t_user = [None, None]
    x2_user = [(), ()]
    prev_tap = [None]
    out_store = [None, None]
    ffn2_bank_user = [None, None, None, None]
    prev_sqC = None
    ld1s = [None] * (NTGC + 1)
    ld2s = [None] * (NTGC + 1)

    def c_loads(g):
        if g >= NTGC:
            return
        gb = g % 2
        x2Tt = bf(C_x2T + gb * 8 * TGC, 8 * TGC)
        x2t = f32v(C_x2 + gb * 4096, 2048)
        ld1s[g] = sp_dma(
            f"sa{gb}",
            [
                (x2Tt[:, TGC * fc : TGC * fc + TGC], x2T_d[fc][:, TGC * g : TGC * g + TGC])
                for fc in range(8)
            ],
            (x2t_user[gb],),
            f"cx2T{g}",
        )
        ld2s[g] = sp_dma(
            f"sx{gb}",
            [(
                x2t.rearrange("p (c f) -> p c f", c=2, f=1024),
                x2_d[TGC * g : TGC * g + TGC, :].rearrange("(c p) f -> p c f", p=128),
            )],
            x2_user[gb],
            f"cx2{g}",
        )

    def mk_f1c(ui):
        g, hc = ui // 32, ui % 32
        gb = g % 2
        x2Tt = bf(C_x2T + gb * 8 * TGC, 8 * TGC)
        hb = ui % 4
        bki = F1BANKS[hb]
        def em_f1(hc=hc, bki=bki, x2Tt=x2Tt):
            r = None
            for fc in range(8):
                r = t.matmul(
                    bank(bki)[:, 0:TGC],
                    w1sb[:, FF * fc + 128 * hc : FF * fc + 128 * hc + 128],
                    x2Tt[:, TGC * fc : TGC * fc + TGC],
                    start=(fc == 0),
                    stop=(fc == 7),
                )
            return r
        deps1 = [ld1s[g], gelu_bank[hb]]
        if ui == 0:
            deps1 += w1_lds
        op = sch.add("pe", em_f1, tuple(deps1), f"f1_{g}.{hc}")
        f1ops[ui] = op
        return op

    def mk_gelu(ui):
        g, hc = ui // 32, ui % 32
        hb = ui % 4
        htile = bf(C_h + hb * TGC, TGC)
        gl = sch.add(
            "act",
            lambda hc=hc, hb=hb, htile=htile: sc.activation(
                htile, bank(F1BANKS[hb])[:, 0:TGC], AF.Gelu, bias=b1f[:, hc : hc + 1]
            ),
            (f1ops[ui], h_user[hb]) + ((eb1,) if ui == 0 else ()),
            f"gelu{g}.{hc}",
        )
        gelu_bank[hb] = gl
        return gl

    c_loads(0)
    c_loads(1)
    f1q = [mk_f1c(0), mk_f1c(1), mk_f1c(2), mk_f1c(3)]
    glq = [mk_gelu(0), mk_gelu(1), mk_gelu(2), mk_gelu(3)]
    NUC = NTGC * 32
    for ui in range(NUC):
        g, hc = ui // 32, ui % 32
        gb = g % 2
        x2t = f32v(C_x2 + gb * 4096, 2048)
        outt = f32v(C_out + gb * 4096, 2048)
        sb_ = 48 * gb
        hb = ui % 4
        htile = bf(C_h + hb * TGC, TGC)
        mm1 = f1q.pop(0)
        gl = glq.pop(0)
        if ui + 4 < NUC:
            f1q.append(mk_f1c(ui + 4))
        def em_f2(hc=hc, htile=htile, first=(hc == 0), last=(hc == 31)):
            r = None
            for tc in range(2):
                for oh in range(2):
                    r = t.matmul(
                        bank(2 + 2 * tc + oh),
                        htile[:, 128 * tc : 128 * tc + 128],
                        w2sb[:, 1024 * hc + 512 * oh : 1024 * hc + 512 * oh + 512],
                        start=first,
                        stop=False,
                    )
            if last:
                for tc in range(2):
                    for oh in range(2):
                        r = t.matmul(
                            bank(2 + 2 * tc + oh),
                            rowbf[0:1, 3072 : 3072 + 128],
                            rowbf[0:1, 4608 + 512 * oh : 4608 + 512 * oh + 512],
                            start=False,
                            stop=True,
                        )
            return r
        deps_f2 = [gl, w2_lds[hc]]
        if hc == 0:
            deps_f2 += [d for d in ffn2_bank_user if d is not None]
        mm2 = sch.add("pe", em_f2, tuple(deps_f2), f"f2_{g}.{hc}")
        h_user[hb] = mm2
        if ui + 4 < NUC:
            glq.append(mk_gelu(ui + 4))
        if hc != 31:
            continue
        x2t_user[gb] = mm1
        ev_ops = []
        for tc in range(2):
            for oh in range(2):
                def em_y3(tc=tc, oh=oh, x2t=x2t, sb_=sb_):
                    return v.scalar_tensor_tensor(
                        x2t[:, 1024 * tc + 512 * oh : 1024 * tc + 512 * oh + 512],
                        bank(2 + 2 * tc + oh),
                        1.0,
                        x2t[:, 1024 * tc + 512 * oh : 1024 * tc + 512 * oh + 512],
                        ALU.mult,
                        ALU.add,
                        accum_out=stats[:, sb_ + 2 * tc + oh : sb_ + 2 * tc + oh + 1],
                    )
                ev = sch.add("dve", em_y3, (mm2, ld2s[g]), f"y3{g}.{tc}.{oh}")
                ffn2_bank_user[2 * tc + oh] = ev
                ev_ops.append(ev)
        stat_ops = []
        sq_ops = []
        for tc in range(2):
            sq = sch.add(
                "act",
                lambda tc=tc, x2t=x2t, sb_=sb_: sc.activation(
                    sq_scrC[:, 0:1024],
                    x2t[:, 1024 * tc : 1024 * tc + 1024],
                    AF.Square,
                    accum_out=stats[:, sb_ + 12 + tc : sb_ + 13 + tc],
                ),
                (ev_ops[2 * tc], ev_ops[2 * tc + 1], prev_sqC),
                f"csq{g}.{tc}",
            )
            prev_sqC = sq
            sq_ops.append(sq)
            s_ = sch.add(
                "dve",
                lambda tc=tc, sb_=sb_: v.scalar_tensor_tensor(
                    stats[:, sb_ + 8 + tc : sb_ + 9 + tc],
                    stats[:, sb_ + 2 * tc : sb_ + 2 * tc + 1],
                    1.0,
                    stats[:, sb_ + 2 * tc + 1 : sb_ + 2 * tc + 2],
                    ALU.mult,
                    ALU.add,
                ),
                (ev_ops[2 * tc], ev_ops[2 * tc + 1]),
                f"cs{g}.{tc}",
            )
            m_ = sch.add(
                "dve",
                lambda tc=tc, sb_=sb_: v.tensor_scalar_mul(
                    stats[:, sb_ + 16 + tc : sb_ + 17 + tc],
                    stats[:, sb_ + 8 + tc : sb_ + 9 + tc],
                    1.0 / D,
                ),
                (s_,),
                f"cm{g}.{tc}",
            )
            mm2_ = sch.add(
                "dve",
                lambda tc=tc, sb_=sb_: v.scalar_tensor_tensor(
                    stats[:, sb_ + 20 + tc : sb_ + 21 + tc],
                    stats[:, sb_ + 16 + tc : sb_ + 17 + tc],
                    -1.0,
                    stats[:, sb_ + 16 + tc : sb_ + 17 + tc],
                    ALU.mult,
                    ALU.mult,
                ),
                (m_,),
                f"cmm{g}.{tc}",
            )
            var2 = sch.add(
                "dve",
                lambda tc=tc, sb_=sb_: v.scalar_tensor_tensor(
                    stats[:, sb_ + 24 + tc : sb_ + 25 + tc],
                    stats[:, sb_ + 12 + tc : sb_ + 13 + tc],
                    1.0 / D,
                    stats[:, sb_ + 20 + tc : sb_ + 21 + tc],
                    ALU.mult,
                    ALU.add,
                ),
                (mm2_, sq),
                f"cvar{g}.{tc}",
            )
            stat_ops.append(var2)
        ln_ = sch.add(
            "act",
            lambda sb_=sb_: sc.activation(
                stats[:, sb_ + 28 : sb_ + 30], stats[:, sb_ + 24 : sb_ + 26],
                AF.Ln, bias=EPS,
            ),
            tuple(stat_ops),
            f"cln{g}",
        )
        rstd_ = sch.add(
            "act",
            lambda sb_=sb_: sc.activation(
                stats[:, sb_ + 32 : sb_ + 34], stats[:, sb_ + 28 : sb_ + 30],
                AF.Exp, scale=-0.5,
            ),
            (ln_,),
            f"crstd{g}",
        )
        nmr_ = sch.add(
            "dve",
            lambda sb_=sb_: v.scalar_tensor_tensor(
                stats[:, sb_ + 36 : sb_ + 38], stats[:, sb_ + 16 : sb_ + 18],
                -1.0, stats[:, sb_ + 32 : sb_ + 34],
                ALU.mult, ALU.mult,
            ),
            (rstd_,),
            f"cnmr{g}",
        )
        ap_ops = []
        for tc in range(2):
            ap1 = sch.add(
                "act",
                lambda tc=tc, x2t=x2t, outt=outt, sb_=sb_: sc.activation(
                    outt[:, 1024 * tc : 1024 * tc + 1024],
                    x2t[:, 1024 * tc : 1024 * tc + 1024],
                    AF.Identity,
                    bias=stats[:, sb_ + 36 + tc : sb_ + 37 + tc],
                    scale=stats[:, sb_ + 32 + tc : sb_ + 33 + tc],
                ),
                (nmr_, out_store[gb], sq_ops[-1]),
                f"cap{g}.{tc}",
            )
            ap_ops.append(ap1)
        x2_user[gb] = tuple(ev_ops + sq_ops + ap_ops)
        if debug_taps:
            tapop = gp_dma(
                "pbias",
                [(
                    y3_tap[TGC * g : TGC * g + TGC, :].rearrange("(c p) f -> p c f", p=128),
                    x2t.rearrange("p (c f) -> p c f", c=2, f=1024),
                )],
                tuple(ev_ops) + ((prev_tap[0],) if prev_tap[0] is not None else ()),
                f"y3tap{g}",
            )
            prev_tap[0] = tapop
            x2_user[gb] = x2_user[gb] + (tapop,)
        c_loads(g + 2)
        st = gp_dma(
            f"pa{g % 2}",
            [(
                out_d[TGC * g : TGC * g + TGC, :].rearrange("(c p) f -> p c f", p=128),
                outt.rearrange("p (c f) -> p c f", c=2, f=1024),
            )],
            tuple(ap_ops),
            f"outst{g}",
        )
        out_store[gb] = st

    sch.finalize()
    with nc.Block() as blk:
        @blk.tensor
        def _(eng):
            sch.emit_engine("pe", eng)

        @blk.scalar
        def _(eng):
            sch.emit_engine("act", eng)

        @blk.vector
        def _(eng):
            sch.emit_engine("dve", eng)

        @blk.gpsimd
        def _(eng):
            sch.emit_engine("pool", eng)

        @blk.sync
        def _(eng):
            sch.emit_engine("sp", eng)

    return nc


def make_in_maps(inputs, n_cores=8):
    x = np.ascontiguousarray(inputs["L0_attn_hidden_states"])
    shared = {
        "wqkv": np.ascontiguousarray(inputs["L0_attn_c_attn_kernel"]),
        "bqkv": np.ascontiguousarray(inputs["L0_attn_c_attn_bias"]),
        "wp": np.ascontiguousarray(inputs["L0_attn_c_proj_kernel"]),
        "bp": np.ascontiguousarray(inputs["L0_attn_c_proj_bias"]),
        "w1": np.ascontiguousarray(inputs["L0_ffn_w1"]),
        "b1": np.ascontiguousarray(inputs["L0_ffn_b1"]),
        "w2": np.ascontiguousarray(inputs["L0_ffn_w2"]),
        "b2": np.ascontiguousarray(inputs["L0_ffn_b2"]),
    }
    return [dict(shared, x=np.ascontiguousarray(x[i])) for i in range(n_cores)]


def kernel(**inputs):
    n_cores = 8
    nc = build_nc(S=4096)
    in_maps = make_in_maps(inputs, n_cores)
    res = run_bass_kernel_spmd(nc, in_maps, core_ids=list(range(n_cores)))
    return np.stack([r["out"] for r in res.results], axis=0)


# version 36 (final)
# speedup vs baseline: 1.1810x; 1.0077x over previous
import numpy as np
import ml_dtypes

import concourse.bass as bass
import concourse.mybir as mybir
from concourse.bass_utils import run_bass_kernel_spmd

F32 = mybir.dt.float32
BF16 = mybir.dt.bfloat16
AF = mybir.ActivationFunctionType
ALU = mybir.AluOpType
AX = mybir.AxisListType

D = 1024
FF = 4096
NPAIR = 8
TGA = 512
TGC = 256
EPS = 1e-5


class _Op:
    __slots__ = ("eng", "sem", "emit", "deps", "n_inc", "val", "tag", "is_dma")

    def __init__(self, eng, sem, emit, deps, n_inc, tag, is_dma):
        self.eng = eng
        self.sem = sem
        self.emit = emit
        self.deps = deps
        self.n_inc = n_inc
        self.val = None
        self.tag = tag
        self.is_dma = is_dma


class Sched:

    def __init__(self, nc):
        self.nc = nc
        self.ops = []
        self.sems = {}
        for name in ("pe", "act", "dve"):
            self._sem(name)
        self._barrier_tokens = None
        self._barrier_seen = None

    def _sem(self, name):
        if name not in self.sems:
            self.sems[name] = self.nc.alloc_semaphore("sem_" + name)
        return self.sems[name]

    def add(self, eng, emit, deps=(), tag="", n_dma=0, sem=None):
        sem = sem or eng
        self._sem(sem)
        deps = [d for d in deps if d is not None]
        if self._barrier_tokens is not None and eng not in self._barrier_seen:
            deps = list(self._barrier_tokens) + deps
            self._barrier_seen.add(eng)
            if len(self._barrier_seen) == 5:
                self._barrier_tokens = None
        is_dma = n_dma > 0
        op = _Op(eng, sem, emit, deps, 16 * n_dma if is_dma else 1, tag, is_dma)
        self.ops.append(op)
        return op

    def barrier(self):
        last = {}
        for op in self.ops:
            last[op.sem] = op
        self._barrier_tokens = list(last.values())
        self._barrier_seen = set()

    def finalize(self):
        cnt = {k: 0 for k in self.sems}
        for op in self.ops:
            cnt[op.sem] += op.n_inc
            op.val = cnt[op.sem]
        self.final = cnt

    def emit_engine(self, name, eng):
        sems = self.sems
        wm = {}
        for op in self.ops:
            if op.eng != name:
                continue
            for d in op.deps:
                if d.eng == "pe" and name == "pe":
                    continue
                v = d.val
                if wm.get(d.sem, 0) >= v:
                    continue
                eng.wait_ge(sems[d.sem], v)
                wm[d.sem] = v
            last = op.emit()
            if not op.is_dma:
                assert last is not None, f"op {op.tag} returned no instruction"
                last.then_inc(sems[op.sem], 1)
        if name == "pool":
            for sname, val in sorted(self.final.items()):
                if val > 0 and sname not in ("pe", "act", "dve"):
                    eng.wait_ge(sems[sname], val)


def build_nc(S=4096, debug_taps=False):
    NTG = S // TGA
    NKT = S // 128
    NTT = S // 128
    NTGC = S // TGC

    nc = bass.Bass("TRN2", target_bir_lowering=False, debug=False)

    x_d = nc.dram_tensor("x", [S, D], F32, kind="ExternalInput").ap()
    wqkv_d = nc.dram_tensor("wqkv", [3 * D, D], F32, kind="ExternalInput").ap()
    bqkv_d = nc.dram_tensor("bqkv", [3 * D], F32, kind="ExternalInput").ap()
    wp_d = nc.dram_tensor("wp", [D, D], F32, kind="ExternalInput").ap()
    bp_d = nc.dram_tensor("bp", [D], F32, kind="ExternalInput").ap()
    w1_d = nc.dram_tensor("w1", [D, FF], F32, kind="ExternalInput").ap()
    b1_d = nc.dram_tensor("b1", [FF], F32, kind="ExternalInput").ap()
    w2_d = nc.dram_tensor("w2", [FF, D], F32, kind="ExternalInput").ap()
    b2_d = nc.dram_tensor("b2", [D], F32, kind="ExternalInput").ap()
    out_d = nc.dram_tensor("out", [S, D], F32, kind="ExternalOutput").ap()

    skind = "ExternalOutput" if debug_taps else "Internal"
    attn_d = nc.dram_tensor("attn_s", [NPAIR, 128, S], BF16, kind=skind).ap()
    x2T_d = nc.dram_tensor("x2T_s", [8, 128, S], BF16, kind=skind).ap()
    x2_d = nc.dram_tensor("x2_s", [S, D], F32, kind=skind).ap()
    r_d = nc.dram_tensor("r_s", [2, 1024], F32, kind="Internal").ap()
    rtmp_d = nc.dram_tensor("rtmp_s", [2, 1024], F32, kind="Internal").ap()
    if debug_taps:
        qkv_tap = nc.dram_tensor("qkv_tap", [3, 128, S], BF16, kind="ExternalOutput").ap()
        y3_tap = nc.dram_tensor("y3_tap", [S, D], F32, kind="ExternalOutput").ap()

    idf_d = nc.inline_tensor(np.eye(128, dtype=np.float32), "idf")
    idb_d = nc.inline_tensor(
        np.eye(128, dtype=np.float32).astype(ml_dtypes.bfloat16), "idb"
    )
    mk = np.zeros((128, 2048), dtype=np.float32)
    jj = np.arange(512)[None, :]
    ii = np.arange(128)[:, None]
    for d_ in range(4):
        mk[:, 512 * d_ : 512 * d_ + 512] = (jj - ii - 128 * d_ >= 0).astype(np.float32)
    masks_d = nc.inline_tensor(mk.astype(ml_dtypes.bfloat16), "cmask")

    idf = nc.alloc_sbuf_tensor("idf_s", [128, 128], F32).ap()
    idb = nc.alloc_sbuf_tensor("idb_s", [128, 128], BF16).ap()
    masks = nc.alloc_sbuf_tensor("masks_s", [128, 2048], BF16).ap()
    ones_bf = nc.alloc_sbuf_tensor("ones_bf", [128, 512], BF16).ap()
    ones_f = nc.alloc_sbuf_tensor("ones_f", [128, 128], F32).ap()
    rowbf = nc.alloc_sbuf_tensor("rowbf", [128, 5632], BF16).ap()
    rowf = nc.alloc_sbuf_tensor("rowf", [128, 1024], F32).ap()
    rb = nc.alloc_sbuf_tensor("rb", [64, 2048], F32).ap()
    stats = nc.alloc_sbuf_tensor("stats", [128, 96], F32).ap()
    aq = nc.alloc_sbuf_tensor("aq", [128, 16], F32).ap()
    b1nat = nc.alloc_sbuf_tensor("b1nat", [32, 128], F32).ap()
    b1f = nc.alloc_sbuf_tensor("b1f", [128, 32], F32).ap()
    cbt = nc.alloc_sbuf_tensor("cbt", [128, 2], F32).ap()
    cvals = nc.alloc_sbuf_tensor("cvals", [128, 4], F32).ap()
    nc.const_aps.register(F32, EPS, cvals[:, 0:1])
    nc.const_aps.register(F32, 0.125, cvals[:, 1:2])
    nc.const_aps.register(F32, -0.5, cvals[:, 2:3])

    avail = nc.sbuf_bytes_remaining
    AR_COLS = (avail - 1024) // 2
    AR = nc.alloc_sbuf_tensor("arena", [128, AR_COLS], BF16).ap()

    def bf(off, n):
        assert off + n <= AR_COLS, f"arena overflow {off}+{n} > {AR_COLS}"
        return AR[:, off : off + n]

    def f32v(off, n):
        assert off % 2 == 0 and off + 2 * n <= AR_COLS, f"arena f32 ovf {off}"
        return AR[:, off : off + 2 * n].bitcast(F32)

    A_xT = 0
    A_q2 = A_xT + 8 * S
    A_k2 = A_q2 + 2 * S
    A_va = A_k2 + 2 * S
    A_wqT = A_va + 3 * 2 * 65 * NKT
    A_wvT = A_wqT + 2 * 2048
    A_pb = A_wvT + 2048
    A_attnC = A_pb + 2 * 1024
    A_xstage = A_attnC + S
    A_wcast = A_xstage + 4096
    A_sq = A_wcast + 3072
    A_u = A_sq + 1024
    A_END = A_u + 3 * 2048
    assert A_END <= AR_COLS, f"A map {A_END} > {AR_COLS}"

    B_w1 = 0
    B_attn = B_w1 + 8 * FF
    B_wpT = B_attn + 2 * 8 * TGA
    B_x2b = B_wpT + 8 * 1024
    B_x2T = B_x2b + 4096
    B_wc = B_x2T
    B_xtg = B_x2T + 8 * TGA
    B_y2 = B_xtg + 8192
    B_x2 = B_y2 + 2 * 8192
    B_END = B_x2 + 8192
    assert B_END <= AR_COLS, f"B map {B_END} > {AR_COLS}"

    C_w1 = 0
    C_w2 = C_w1 + 8 * FF
    C_h = C_w2 + 32 * 1024
    C_x2T = C_h + 4 * TGC
    C_x2 = C_x2T + 2 * 8 * TGC
    C_out = C_x2 + 2 * 4096
    C_scr = C_out + 2 * 4096
    C_END = C_scr + 2048
    assert C_END <= AR_COLS, f"C map {C_END} > {AR_COLS}"

    PS = nc.alloc_psum_tensor("ps", [128, 4096], F32).ap()

    def bank(i):
        return PS[:, 512 * i : 512 * i + 512]

    def bank_bf(i):
        return PS[:, 512 * i : 512 * i + 512].bitcast(BF16)

    sch = Sched(nc)
    t = nc.tensor
    v = nc.vector
    sc = nc.scalar
    sp = nc.sync
    gp = nc.gpsimd

    def sp_dma(stream, outs_ins, deps=(), tag=""):
        sem = sch._sem(stream)

        def em():
            for o, i in outs_ins:
                sp.dma_start(out=o, in_=i).then_inc(sem, 16)
        return sch.add("sp", em, deps, tag, n_dma=len(outs_ins), sem=stream)

    def gp_dma(stream, outs_ins, deps=(), tag=""):
        sem = sch._sem(stream)

        def em():
            for o, i in outs_ins:
                gp.dma_start(out=o, in_=i).then_inc(sem, 16)
        return sch.add("pool", em, deps, tag, n_dma=len(outs_ins), sem=stream)

    c0 = sp_dma(
        "spc",
        [
            (idf, idf_d.ap()),
            (idb, idb_d.ap()),
            (masks, masks_d.ap()),
            (b1nat, b1_d.rearrange("(c p) -> c p", p=128)),
        ],
        tag="consts",
    )
    m0a = sch.add("dve", lambda: v.memset(cvals[:, 0:1], EPS), (), "c_eps")
    m0b = sch.add("dve", lambda: v.memset(cvals[:, 1:2], 0.125), (m0a,), "c_s8")
    m0c = sch.add("dve", lambda: v.memset(cvals[:, 2:3], -0.5), (m0b,), "c_mh")
    m1 = sch.add("dve", lambda: v.memset(ones_bf, 1.0), (m0c,), "ones_bf")
    m2 = sch.add("dve", lambda: v.memset(ones_f, 1.0), (m1,), "ones_f")
    m3 = sch.add("dve", lambda: v.memset(rowbf[0:1, 3072:3584], 1.0), (m2,), "ones_row")
    va_both = bf(A_va, 3 * 2 * 65 * NKT)
    m4 = sch.add(
        "dve",
        lambda: v.memset(va_both.rearrange("p (a c) -> p a c", c=65)[:, :, 64:65], 1.0),
        (m3,),
        "va_ones",
    )
    cbias = gp_dma(
        "pbias",
        [
            (rowbf[0:1, 0:3072], bqkv_d[None, :]),
            (rowbf[0:1, 3584:4608], bp_d[None, :]),
            (rowbf[0:1, 4608:5632], b2_d[None, :]),
        ],
        tag="bias_cast",
    )
    tb1 = sch.add(
        "pe",
        lambda: t.transpose(PS[0:128, 2048:2080], b1nat, idf[0:32, 0:32]),
        (c0,),
        "b1T",
    )
    eb1 = sch.add("dve", lambda: v.tensor_copy(b1f, PS[0:128, 2048:2080]), (tb1,), "b1Tev")

    sch.barrier()

    xT = bf(A_xT, 8 * S)
    xT4 = xT.rearrange("p (f t c) -> p f t c", f=8, t=NTT, c=128)
    xstage_b = bf(A_xstage, 4096)
    last_ev = {0: None, 1: None}
    xT_ready = [None] * NTT

    xTb = xT
    wcastA = bf(A_wcast, 3072)

    prev_exp = [None, None]
    attnC_store = [None]
    attn_store_par = [None, None]
    prev_pair_uev = None
    prev_pair_nm = None
    pbuf_user = [None, None]
    bank45_user = {0: eb1, 1: eb1}
    str_prev = [None, None]
    ldw_prev = [None, None]
    rcw_prev = [None, None]
    prev_cbv = [None]
    nm_par = [None, None, None]
    nm_par2 = [None, None]
    qsq_user = [None, None]
    pending_vevs = {}
    handles = [None] * NPAIR
    wld_pending = {}
    vld_pending = {}
    vmm_defer = {}

    def make_start(p):
        pb = p % 2
        H_ = {
            "q2": bf(A_q2 + pb * S, S),
            "k2": bf(A_k2 + pb * S, S),
            "va": bf(A_va + (p % 3) * 2 * 65 * NKT, 2 * 65 * NKT),
            "wqT": bf(A_wqT + pb * 2048, 1024),
            "wkT": bf(A_wqT + pb * 2048 + 1024, 1024),
            "qk_ev": {},
            "qk_mma": {},
            "wev": [],
            "wtr": [],
            "vwtr": [],
            "cbcol": cbt[:, pb : pb + 1],
            "cbe": None,
        }
        if p % 2 == 0:
            H_["va_evs"] = []
            pending_vevs[p + 1] = []
        else:
            H_["va_evs"] = pending_vevs.pop(p)
        handles[p] = H_
        H_["va3"] = H_["va"].rearrange("p (h a c) -> p h a c", h=2, a=NKT, c=65)
        thunks = []
        nrm_toks = {}
        H_["wld"] = []

        def t_wld(wi):
            row0 = [128 * p, D + 128 * p][wi]
            stg = wcastA[:, 1024 * wi : 1024 * wi + 1024]
            ld = gp_dma(f"pw{wi}", [(stg, wqkv_d[row0 : row0 + 128, :])],
                        (), f"wld{p}.{wi}")
            H_["wld"].append(ld)

        def t_wld_next(wi):
            row0 = [128 * (p + 1), D + 128 * (p + 1)][wi]
            stg = wcastA[:, 1024 * wi : 1024 * wi + 1024]
            ld = gp_dma(f"pw{wi}", [(stg, wqkv_d[row0 : row0 + 128, :])],
                        (H_["wtr"][wi],), f"wld{p + 1}.{wi}")
            wld_pending[p + 1].append(ld)

        def t_wtr(wi):
            wdstT = [H_["wqT"], H_["wkT"]][wi]
            stg = wcastA[:, 1024 * wi : 1024 * wi + 1024]
            bki = 4 + wi % 2
            def em_wtr(stg=stg, bki=bki):
                r = None
                for k in range(8):
                    r = t.transpose(
                        bank_bf(bki)[:, 128 * k : 128 * k + 128],
                        stg[:, 128 * k : 128 * k + 128],
                        idb,
                    )
                return r
            tr = sch.add("pe", em_wtr, (H_["wld"][wi], bank45_user[wi % 2]), f"wT{p}.{wi}")
            def em_wev(wdstT=wdstT, bki=bki):
                return v.tensor_copy(wdstT, bank_bf(bki)[:, 0:1024])
            ev = sch.add("dve", em_wev, (tr,), f"wTev{p}.{wi}")
            H_["wev"].append(ev)
            H_["wtr"].append(tr)
            bank45_user[wi % 2] = ev

        if p == 0:
            for wi in range(2):
                thunks.append((-1, lambda wi=wi: t_wld(wi)))
        else:
            H_["wld"] = wld_pending.pop(p)
        for wi in range(2):
            thunks.append((0, lambda wi=wi: t_wtr(wi)))
        if p + 1 < NPAIR:
            wld_pending[p + 1] = []
            for wi in range(2):
                thunks.append((-1, lambda wi=wi: t_wld_next(wi)))

        def t_qk(qi, g):
            wT = [H_["wqT"], H_["wkT"]][qi]
            dst = [H_["q2"], H_["k2"]][qi]
            brow = [128 * p, D + 128 * p][qi]
            idx = qi * NTG + g
            bki = 4 + idx % 2
            def em_qk_a(wT=wT, g=g, bki=bki):
                r = None
                for fc in range(4):
                    r = t.matmul(
                        bank(bki),
                        wT[:, 128 * fc : 128 * fc + 128],
                        xTb[:, S * fc + TGA * g : S * fc + TGA * g + TGA],
                        start=(fc == 0),
                        stop=False,
                    )
                return r
            def em_qk_b(wT=wT, g=g, bki=bki, brow=brow):
                for fc in range(4, 8):
                    t.matmul(
                        bank(bki),
                        wT[:, 128 * fc : 128 * fc + 128],
                        xTb[:, S * fc + TGA * g : S * fc + TGA * g + TGA],
                        start=False,
                        stop=False,
                    )
                return t.matmul(
                    bank(bki),
                    rowbf[0:1, brow : brow + 128],
                    rowbf[0:1, 3072 : 3072 + TGA],
                    start=False,
                    stop=True,
                )
            mma = sch.add(
                "pe", em_qk_a,
                (H_["wev"][qi], bank45_user[idx % 2], m3, cbias,
                 xT_ready[min(4 * g + 3, NTT - 1)]),
                f"qka{p}.{qi}.{g}",
            )
            H_["qk_mma"][(qi, g)] = (mma, em_qk_b, dst, bki)

        def t_qk_b(qi, g):
            idx = qi * NTG + g
            (mma, em_qk_b, dst, bki) = H_["qk_mma"][(qi, g)]
            mm = sch.add("pe", em_qk_b, (mma,), f"qk{p}.{qi}.{g}")
            def em_ev(dst=dst, g=g, bki=bki):
                return v.tensor_copy(dst[:, TGA * g : TGA * g + TGA], bank(bki))
            ev = sch.add("dve", em_ev, (mm,), f"qkev{p}.{qi}.{g}")
            bank45_user[idx % 2] = ev
            H_["qk_ev"][(qi, g)] = ev

        def t_nrm(qi, g):
            srcb = [H_["q2"], H_["k2"]][qi]
            sl = g % 2
            qsq = bf(A_sq + sl * 512, 512)
            sqg = sch.add(
                "dve",
                lambda srcb=srcb, g=g, qsq=qsq: v.tensor_mul(
                    qsq, srcb[:, TGA * g : TGA * g + TGA], srcb[:, TGA * g : TGA * g + TGA]
                ),
                (H_["qk_ev"][(qi, g)], prev_cbv[0], qsq_user[sl]),
                f"nsq{p}.{qi}.{g}",
            )
            bki = 4 + g % 2
            mmn = sch.add(
                "pe",
                lambda qsq=qsq, bki=bki: t.matmul(
                    bank(bki)[0:1, :],
                    ones_bf[0:128, 0:1],
                    qsq,
                    start=True,
                    stop=True,
                ),
                (sqg, bank45_user[g % 2]),
                f"nmm{p}.{qi}.{g}",
            )
            qsq_user[sl] = mmn
            rd = sch.add(
                "dve",
                lambda qi=qi, g=g, bki=bki: v.reduce_max(
                    stats[0:1, 40 + 8 * qi + g : 41 + 8 * qi + g],
                    bank(bki)[0:1, :],
                    axis=AX.X,
                ),
                (mmn,),
                f"nrd{p}.{qi}.{g}",
            )
            bank45_user[g % 2] = rd
            nrm_toks[(qi, g)] = rd

        for g in range(NTG):
            for qi in range(2):
                thunks.append((4 * g + 3, lambda qi=qi, g=g: t_qk(qi, g)))
                thunks.append((4 * g + 3, lambda qi=qi, g=g: t_qk_b(qi, g)))
            for qi in range(2):
                thunks.append((4 * g + 3, lambda qi=qi, g=g: t_nrm(qi, g)))

        def t_cb():
            qn2 = sch.add(
                "dve",
                lambda: v.reduce_max(stats[0:1, 38:39], stats[0:1, 40 : 40 + NTG], axis=AX.X),
                tuple(nrm_toks[(0, g)] for g in range(NTG)),
                f"qn2{p}",
            )
            kn2 = sch.add(
                "dve",
                lambda: v.reduce_max(stats[0:1, 39:40], stats[0:1, 48 : 48 + NTG], axis=AX.X),
                tuple(nrm_toks[(1, g)] for g in range(NTG)),
                f"kn2{p}",
            )
            cb0 = sch.add(
                "dve",
                lambda: v.scalar_tensor_tensor(
                    stats[0:1, 37:38], stats[0:1, 38:39], 1.0, stats[0:1, 39:40],
                    ALU.mult, ALU.add,
                ),
                (qn2, kn2),
                f"cb0{p}",
            )
            cbv = sch.add(
                "dve",
                lambda: v.tensor_scalar_mul(stats[0:1, 36:37], stats[0:1, 37:38], -0.0625),
                (cb0,),
                f"cbv{p}",
            )
            prev_cbv[0] = cbv
            cbm = sch.add(
                "pe",
                lambda: t.matmul(
                    PS[0:128, 2176:2177], ones_f[0:1, 0:128], stats[0:1, 36:37],
                    start=True, stop=True,
                ),
                (cbv, bank45_user[0]),
                f"cbm{p}",
            )
            cbe = sch.add(
                "dve",
                lambda: v.tensor_copy(H_["cbcol"], PS[0:128, 2176:2177]),
                (cbm, prev_exp[0], prev_exp[1]),
                f"cbe{p}",
            )
            H_["cbe"] = cbe
            bank45_user[0] = cbe

        thunks.append((NKT - 1, t_cb))

        if p % 2 == 1 and p + 1 < NPAIR:
            vld_pending[p + 1] = []

            def t_vld_next(r):
                row0 = 2 * D + 128 * (p + 1) + 128 * r
                stg = xstage_b[:, 1024 * r : 1024 * r + 1024]
                if p + 1 == 2:
                    dep = (handles[0]["vwtr"][1], xT_ready[NTT - 1])
                else:
                    dep = (handles[p - 1]["vwtr"][r],)
                ld = gp_dma("pw2" if r == 0 else "pw3",
                            [(stg, wqkv_d[row0 : row0 + 128, :])],
                            dep, f"vwld{p + 1}.{r}")
                vld_pending[p + 1].append(ld)

            for r_ in (1, 0):
                thunks.insert(0, (-1, lambda r=r_: t_vld_next(r)))
        if p % 2 == 1:
            for i_, th_ in enumerate(vmm_defer.pop(p, [])):
                thunks.insert(6 + i_, th_)

        if p % 2 == 0:
            wvg = bf(A_wvT, 2048)
            va_next = bf(A_va + ((p + 1) % 3) * 2 * 65 * NKT, 2 * 65 * NKT)
            va3_next = va_next.rearrange("p (h a c) -> p h a c", h=2, a=NKT, c=65)
            vevs_next = pending_vevs[p + 1]
            vstate = {"wld": [] if p == 0 else vld_pending.pop(p)}

            def t_vld(r):
                row0 = 2 * D + 128 * p + 128 * r
                stg = wcastA[:, 2048:3072]
                dep_stage = H_["vwtr"][r - 1] if r >= 1 else None
                ld = gp_dma("pw2", [(stg, wqkv_d[row0 : row0 + 128, :])],
                            (dep_stage,), f"vwld{p}.{r}")
                vstate["wld"].append(ld)

            def t_vtr(r):
                if p == 0:
                    stg = wcastA[:, 2048:3072]
                else:
                    stg = xstage_b[:, 1024 * r : 1024 * r + 1024]
                bki = 4 + r % 2
                def em_wtr(stg=stg, bki=bki):
                    rr = None
                    for k in range(8):
                        rr = t.transpose(
                            bank_bf(bki)[:, 128 * k : 128 * k + 128],
                            stg[:, 128 * k : 128 * k + 128],
                            idb,
                        )
                    return rr
                tr = sch.add("pe", em_wtr, (vstate["wld"][r], bank45_user[r % 2]), f"vwT{p}.{r}")
                def em_wev(r=r, bki=bki):
                    return v.tensor_copy(
                        wvg.rearrange("p (f pr c) -> p f pr c", f=8, pr=2, c=128)[:, :, r, :],
                        bank_bf(bki).rearrange("p (a c) -> p a c", a=8, c=128),
                    )
                ev = sch.add("dve", em_wev, (tr,), f"vwTev{p}.{r}")
                H_["vwtr"].append(ev)
                bank45_user[r % 2] = ev

            def mk_vg(a):
                par = a % 2
                bki = 4 + par
                def em_v_a(a=a, bki=bki):
                    r = None
                    for fc in range(4):
                        r = t.matmul(
                            bank(bki)[:, 0:256],
                            xTb[:, S * fc + 128 * a : S * fc + 128 * a + 128],
                            wvg[:, 256 * fc : 256 * fc + 256],
                            start=(fc == 0),
                            stop=False,
                        )
                    return r
                def em_v_b(a=a, bki=bki):
                    for fc in range(4, 8):
                        t.matmul(
                            bank(bki)[:, 0:256],
                            xTb[:, S * fc + 128 * a : S * fc + 128 * a + 128],
                            wvg[:, 256 * fc : 256 * fc + 256],
                            start=False,
                            stop=False,
                        )
                    return t.matmul(
                        bank(bki)[:, 0:256],
                        rowbf[0:1, 3072 : 3072 + 128],
                        rowbf[0:1, 2 * D + 128 * p : 2 * D + 128 * p + 256],
                        start=False,
                        stop=True,
                    )
                mma = sch.add(
                    "pe", em_v_a,
                    (H_["vwtr"][1], bank45_user[par], m4, cbias,
                     xT_ready[min(a, NTT - 1)]),
                    f"va{p}.{a}",
                )
                vstate[("mma", a)] = (mma, em_v_b)

            def mk_vg_b(a):
                par = a % 2
                bki = 4 + par
                (mma, em_v_b) = vstate[("mma", a)]
                mm = sch.add("pe", em_v_b, (mma,), f"v{p}.{a}")
                def em_vev(a=a, bki=bki):
                    return v.tensor_copy(
                        H_["va3"][:, :, a, 0:64],
                        bank(bki)[:, 0:128].rearrange("p (h c) -> p h c", h=2, c=64),
                    )
                ev = sch.add("dve", em_vev, (mm,), f"vev{p}.{a}")
                H_["va_evs"].append(ev)
                def em_vev2(a=a, bki=bki):
                    return v.tensor_copy(
                        va3_next[:, :, a, 0:64],
                        bank(bki)[:, 128:256].rearrange("p (h c) -> p h c", h=2, c=64),
                    )
                ev2 = sch.add("dve", em_vev2, (ev, mm), f"vev2{p}.{a}")
                bank45_user[par] = ev2
                vevs_next.append(ev2)

            if p == 0:
                thunks.append((-1, lambda: t_vld(0)))
                thunks.append((0, lambda: t_vtr(0)))
                thunks.append((-1, lambda: t_vld(1)))
                thunks.append((1, lambda: t_vtr(1)))
            else:
                thunks.append((0, lambda: t_vtr(0)))
                thunks.append((1, lambda: t_vtr(1)))
            a_defer = NKT
            for a in range(a_defer):
                thunks.append((a, lambda a=a: mk_vg(a)))
                thunks.append((a, lambda a=a: mk_vg_b(a)))
            dl = []
            for a in range(a_defer, NKT):
                dl.append((a, lambda a=a: mk_vg(a)))
                dl.append((a, lambda a=a: mk_vg_b(a)))
            vmm_defer[p + 1] = dl
        return thunks

    thunks0 = make_start(0)
    t0i = 0
    for tp_ in range(NTT // 2):
        b_ = tp_ % 2
        stg = xstage_b[:, 2048 * b_ : 2048 * b_ + 2048]
        ld = gp_dma(
            f"px{b_}",
            [(
                stg.rearrange("p (c f) -> p c f", c=2, f=1024),
                x_d[256 * tp_ : 256 * tp_ + 256, :].rearrange("(c p) f -> p c f", p=128),
            )],
            (last_ev[b_],),
            f"xld{tp_}",
        )
        def em_tr(stg=stg, b_=b_):
            r = None
            for k in range(16):
                r = t.transpose(
                    PS[:, 1024 * b_ : 1024 * b_ + 1024].bitcast(BF16)[:, 128 * k : 128 * k + 128],
                    stg[:, 128 * k : 128 * k + 128],
                    idb,
                )
            return r
        tr = sch.add("pe", em_tr, (ld, last_ev[b_]), f"xT{tp_}")
        def em_ev(tp_=tp_, b_=b_):
            return v.tensor_copy(
                xT4[:, :, 2 * tp_ : 2 * tp_ + 2, :].transpose([0, 2, 1, 3]),
                PS[:, 1024 * b_ : 1024 * b_ + 1024]
                .bitcast(BF16)
                .rearrange("p (c a e) -> p c a e", c=2, a=8, e=128),
            )
        ev = sch.add("dve", em_ev, (tr,), f"xTev{tp_}")
        last_ev[b_] = ev
        xT_ready[2 * tp_] = ev
        xT_ready[2 * tp_ + 1] = ev
        while t0i < len(thunks0) and thunks0[t0i][0] <= 2 * tp_ + 1:
            thunks0[t0i][1]()
            t0i += 1
    while t0i < len(thunks0):
        thunks0[t0i][1]()
        t0i += 1

    preB_wp = []
    preB_w1 = []
    preB_slots = []
    for p in range(NPAIR):
        H_ = handles[p]
        q2 = H_["q2"]
        k2 = H_["k2"]
        va = H_["va"]
        qk_ev = H_["qk_ev"]
        va_evs = H_["va_evs"]
        cbe = H_["cbe"]
        cbcol = H_["cbcol"]
        attnC = bf(A_attnC, S)
        nxt = make_start(p + 1) if p + 1 < NPAIR else []
        nxt_i = 0
        if p == NPAIR - 1 and S >= FF:
            wp_slots = [
                (wcastA[:, 0:1024], H_["wtr"][0]),
                (wcastA[:, 1024:2048], H_["wtr"][1]),
                (xstage_b[:, 0:1024], handles[6]["vwtr"][0]),
                (xstage_b[:, 1024:2048], handles[6]["vwtr"][1]),
                (xstage_b[:, 2048:3072], xT_ready[NTT - 1]),
                (xstage_b[:, 3072:4096], xT_ready[NTT - 1]),
                (wcastA[:, 2048:3072], handles[0]["vwtr"][1]),
                (bf(A_sq, 1024), H_["cbe"]),
            ]
            for oc in range(8):
                stg, slot_dep = wp_slots[oc]
                chain = preB_wp[oc - 2] if oc >= 2 else None
                preB_wp.append(gp_dma(
                    f"pw{oc % 2}",
                    [(stg, wp_d[128 * oc : 128 * oc + 128, :])],
                    (slot_dep, chain),
                    f"wpld{oc}",
                ))
            preB_slots.extend(s for s, _ in wp_slots)
            if True:
                lastqk = [H_["qk_ev"][(qi, NTG - 1)] for qi in range(2)]
                for fc in range(8):
                    dep = (preB_w1[fc - 2],) if fc >= 2 else tuple(lastqk)
                    preB_w1.append(gp_dma(
                        f"pq{fc % 2}",
                        [(bf(0, 8 * FF)[:, FF * fc : FF * fc + FF],
                          w1_d[128 * fc : 128 * fc + 128, :])],
                        dep,
                        f"w1ld{fc}",
                    ))

        uev_prev = None
        norm_ops = []
        pending_tail = []
        tail_due = []
        units = [(g, j) for g in range(NTG) for j in range(4 * (g + 1))]

        def tail_stage1(gt, attnU, uev):
            g2_ = gt % 2
            dwide = rowf[:, 32 * g2_ : 32 * g2_ + 8]
            st_d = sp_dma(
                f"sr{g2_}",
                [(rtmp_d[g2_ : g2_ + 1, :], attnU[64:65, 0:1024])],
                (uev, ldw_prev[g2_], nm_par2[g2_]),
                f"std{p}.{gt}",
            )
            ld_w = sp_dma(
                f"sw{g2_}",
                [(dwide, rtmp_d[g2_ : g2_ + 1, :].rearrange("o (q c) -> (o q) c", q=128))],
                (st_d, rcw_prev[g2_]),
                f"ldw{p}.{gt}",
            )
            ldw_prev[g2_] = ld_w
            return ld_w

        def flush_tail():
            (gt, attnU, uev, ld_w) = pending_tail.pop(0)
            g3 = gt % 3
            g2_ = gt % 2
            dwide = rowf[:, 32 * g2_ : 32 * g2_ + 8]
            rwide = rowf[:, 32 * g2_ + 8 : 32 * g2_ + 16]
            rc = sch.add(
                "dve",
                lambda dwide=dwide, rwide=rwide: v.reciprocal(rwide, dwide),
                (ld_w, str_prev[g2_]),
                f"rc{p}.{gt}",
            )
            rcw_prev[g2_] = rc
            st_r = sp_dma(
                f"sr{g2_}",
                [(r_d[g2_ : g2_ + 1, :].rearrange("o (q c) -> (o q) c", q=128), rwide)],
                (rc, nm_par2[g2_]),
                f"str{p}.{gt}",
            )
            str_prev[g2_] = st_r
            ld_rb = gp_dma(
                f"pl{g2_}",
                [(
                    rb[0:64, 1024 * g2_ : 1024 * g2_ + 1024],
                    r_d[g2_ : g2_ + 1, :].broadcast_to([64, 1024]),
                )],
                (st_r, nm_par2[g2_]),
                f"ldrb{p}.{gt}",
            )
            rbase = 1024 * g2_
            nm = sch.add(
                "pool",
                lambda gt=gt, attnU=attnU, attnC=attnC, rbase=rbase: (
                    gp.tensor_tensor(
                        attnC[0:64, TGA * gt : TGA * gt + TGA],
                        attnU[0:64, 0:512],
                        rb[0:64, rbase : rbase + 512],
                        ALU.mult,
                    ),
                    gp.tensor_tensor(
                        attnC[64:128, TGA * gt : TGA * gt + TGA],
                        attnU[0:64, 512:1024],
                        rb[0:64, rbase + 512 : rbase + 1024],
                        ALU.mult,
                    ),
                )[-1],
                (ld_rb, attnC_store[0] if len(norm_ops) == 0 else None),
                f"nm{p}.{gt}",
            )
            nm_par[g3] = nm
            nm_par2[g2_] = nm
            norm_ops.append(nm)

        def mk_st(ui):
            g, j = units[ui]
            ub = ui % 2
            def em_st(g=g, j=j, ub=ub, q2=q2, k2=k2):
                r = None
                for hh in range(2):
                    r = t.matmul(
                        PS[:, 1024 * ub + 512 * hh : 1024 * ub + 512 * hh + 512],
                        k2[64 * hh : 64 * hh + 64, 128 * j : 128 * j + 128],
                        q2[64 * hh : 64 * hh + 64, TGA * g : TGA * g + TGA],
                        start=True,
                        stop=True,
                    )
                return r
            return sch.add(
                "pe",
                em_st,
                (prev_exp[ub], qk_ev[(0, g)], qk_ev[(1, g)]),
                f"st{p}.{g}.{j}",
            )

        st_next = mk_st(0)
        for ui, (g, j) in enumerate(units):
            while pending_tail and g >= pending_tail[0][0] + 2:
                flush_tail()
            nunit = 4 * (g + 1)
            ub = ui % 2
            pb3 = ui % 2
            Pb = bf(A_pb + pb3 * 1024, 1024)
            quad = PS[:, 1024 * ub : 1024 * ub + 1024]
            st = st_next
            diag = j >= nunit - 4
            mi = j - (nunit - 4) if diag else 0
            dcol = 128 * mi
            if not diag:
                def em_exp(Pb=Pb, quad=quad, cbcol=cbcol):
                    return sc.activation(Pb, quad, AF.Exp, bias=cbcol, scale=0.125)
                ex = sch.add("act", em_exp, (st, cbe, pbuf_user[pb3]), f"exp{p}.{g}.{j}")
                prev_exp[ub] = ex
                pv_dep = ex
            else:
                ms0 = None
                if dcol > 0:
                    ms0 = sch.add(
                        "dve",
                        lambda Pb=Pb, dcol=dcol: v.memset(
                            Pb.rearrange("p (h c) -> p h c", h=2, c=512)[:, :, 0:dcol], 0.0
                        ),
                        (pbuf_user[pb3],),
                        f"pz{p}.{g}.{j}",
                    )
                def em_exp(Pb=Pb, quad=quad, cbcol=cbcol, dcol=dcol):
                    return sc.activation(
                        Pb.rearrange("p (h c) -> p h c", h=2, c=512)[:, :, dcol:],
                        quad.rearrange("p (h c) -> p h c", h=2, c=512)[:, :, dcol:],
                        AF.Exp, bias=cbcol, scale=0.125,
                    )
                ex = sch.add("act", em_exp, (st, cbe, pbuf_user[pb3]), f"exp{p}.{g}.{j}")
                prev_exp[ub] = ex
                def em_mask(Pb=Pb, dcol=dcol):
                    r = None
                    for hh in range(2):
                        r = v.tensor_tensor(
                            Pb[:, 512 * hh + dcol : 512 * hh + dcol + 128],
                            Pb[:, 512 * hh + dcol : 512 * hh + dcol + 128],
                            masks[:, 0:128],
                            ALU.mult,
                        )
                    return r
                msk = sch.add("dve", em_mask, (ex, ms0), f"msk{p}.{g}.{j}")
                pv_dep = msk
            st_next = mk_st(ui + 1) if ui + 1 < len(units) else None
            while nxt_i < len(nxt) and nxt[nxt_i][0] < 0:
                nxt[nxt_i][1]()
                nxt_i += 1
            if nxt_i < len(nxt) and ui >= 8:
                nxt[nxt_i][1]()
                nxt_i += 1
            def em_pv(j=j, Pb=Pb, va=va, first=(j == 0), last=(j == nunit - 1)):
                r = None
                for hh in range(2):
                    r = t.matmul(
                        PS[0:65, 3072 + 512 * hh : 3584 + 512 * hh],
                        va[:, 65 * (NKT * hh + j) : 65 * (NKT * hh + j) + 65],
                        Pb[:, 512 * hh : 512 * hh + 512],
                        start=first,
                        stop=last,
                    )
                return r
            deps_pv = [pv_dep, va_evs[j]]
            if j == 0:
                deps_pv.append(uev_prev if g > 0 else prev_pair_uev)
            pv = sch.add("pe", em_pv, tuple(deps_pv), f"pv{p}.{g}.{j}")
            pbuf_user[pb3] = pv
            if j != nunit - 1:
                continue
            g3 = g % 3
            attnU = f32v(A_u + g3 * 2048, 1024)
            uev = sch.add(
                "dve",
                lambda attnU=attnU: v.tensor_copy(attnU[0:65, :], PS[0:65, 3072:4096]),
                (pv, nm_par[g3]),
                f"uev{p}.{g}",
            )
            ld_w = tail_stage1(g, attnU, uev)
            pending_tail.append((g, attnU, uev, ld_w))
            uev_prev = uev
        while pending_tail:
            flush_tail()
        while nxt_i < len(nxt):
            nxt[nxt_i][1]()
            nxt_i += 1
        st_op = gp_dma(f"pa{p % 2}", [(attn_d[p], attnC)], tuple(norm_ops), f"attn_st{p}")
        attnC_store[0] = st_op
        attn_store_par[p % 2] = st_op
        prev_pair_uev = uev_prev
        prev_pair_nm = norm_ops[-1]

        if debug_taps and p == 0:
            gp_dma(
                "pbias",
                [
                    (qkv_tap[0], q2),
                    (qkv_tap[1], k2),
                    (qkv_tap[2][:, 0 : 65 * NKT], va[:, 0 : 65 * NKT]),
                ],
                tuple(qk_ev.values()) + tuple(va_evs),
                "qkv_tap",
            )
            sch.barrier()

    lastA = {}
    for _op in sch.ops:
        lastA[_op.eng] = _op


    w1sb = bf(B_w1, 8 * FF)
    wpT = bf(B_wpT, 8 * 1024)
    x2b = bf(B_x2b, 4096)
    sq_scrB = f32v(B_x2b, 2048)
    x2Ttg = bf(B_x2T, 8 * TGA)
    x2T4 = x2Ttg.rearrange("p (f t c) -> p f t c", f=8, t=TGA // 128, c=128)
    wcB = bf(B_wc, 2048)
    x2f = f32v(B_x2, 4096)

    wp_evs = []
    wp_trs = []
    for oc in range(8):
        if preB_wp:
            stg = preB_slots[oc]
            ld = preB_wp[oc]
        elif oc < 2:
            stg = wcB[:, 1024 * (oc % 2) : 1024 * (oc % 2) + 1024]
            ld = gp_dma(f"pw{oc % 2}", [(stg, wp_d[128 * oc : 128 * oc + 128, :])],
                        (lastA["pe"], lastA["dve"], lastA["act"], lastA["pool"]), f"wpld{oc}")
        else:
            stg = wcB[:, 1024 * (oc % 2) : 1024 * (oc % 2) + 1024]
            ld = gp_dma(f"pw{oc % 2}", [(stg, wp_d[128 * oc : 128 * oc + 128, :])], (wp_trs[oc - 2],), f"wpld{oc}")
        def em_tr(stg=stg, oc=oc):
            r = None
            for k in range(8):
                r = t.transpose(
                    bank_bf(2 + oc % 2)[:, 128 * k : 128 * k + 128],
                    stg[:, 128 * k : 128 * k + 128],
                    idb,
                )
            return r
        if oc < 2:
            tr = sch.add("pe", em_tr, (ld, prev_exp[0], prev_exp[1]), f"wpT{oc}")
        else:
            tr = sch.add("pe", em_tr, (ld, wp_evs[oc - 2]), f"wpT{oc}")
        def em_ev(oc=oc):
            return v.tensor_copy(
                wpT.rearrange("p (f o c) -> p f o c", f=8, o=8, c=128)[:, :, oc, :],
                bank_bf(2 + oc % 2).rearrange("p (a c) -> p a c", a=8, c=128),
            )
        ev = sch.add("dve", em_ev, (tr,) + ((lastA["dve"],) if oc < 2 else ()), f"wpTev{oc}")
        wp_evs.append(ev)
        wp_trs.append(tr)

    if preB_w1:
        w1_lds = preB_w1
    else:
        w1_lds = []
        for fc in range(8):
            ld = gp_dma(
                f"pq{fc % 2}",
                [(w1sb[:, FF * fc : FF * fc + FF], w1_d[128 * fc : 128 * fc + 128, :])],
                (w1_lds[fc - 2] if fc >= 2 else None,),
                f"w1ld{fc}",
            )
            w1_lds.append(ld)

    attn_user = [None, None]
    xld_user = [None, None]
    proj_ev_last = {0: None, 1: None}
    x2t_ev_last = {0: None, 1: None}
    b_pending = [None]
    b_st1 = [None]
    b_x2t_trs = [[]]
    prev_st2 = None
    y2_readers_par = [(), ()]
    prev_sq = None
    for g in range(NTG):
        sb_ = 48 * (g % 2)
        atg = bf(B_attn + (g % 2) * 8 * TGA, 8 * TGA)
        xtg = f32v(B_xtg, 4096)
        y2 = f32v(B_y2 + (g % 2) * 8192, 4096)
        joinAB = (
            (lastA["pe"], lastA["dve"], attn_store_par[0], attn_store_par[1])
            if g < 2 else ()
        )
        a_ld = sp_dma(
            f"sa{g % 2}",
            [
                (atg[:, TGA * fc : TGA * fc + TGA], attn_d[fc][:, TGA * g : TGA * g + TGA])
                for fc in range(8)
            ],
            (attn_user[g % 2],) + joinAB,
            f"ald{g}",
        )
        x_ld_a = sp_dma(
            "sx0",
            [(
                xtg[:, 0:2048].rearrange("p (c f) -> p c f", c=2, f=1024),
                x_d[TGA * g : TGA * g + 256, :].rearrange("(c p) f -> p c f", p=128),
            )],
            (xld_user[0],) + joinAB,
            f"xldBa{g}",
        )
        x_ld_b = sp_dma(
            "sx1",
            [(
                xtg[:, 2048:4096].rearrange("p (c f) -> p c f", c=2, f=1024),
                x_d[TGA * g + 256 : TGA * g + TGA, :].rearrange("(c p) f -> p c f", p=128),
            )],
            (xld_user[1],) + joinAB,
            f"xldBb{g}",
        )
        y2_evs = []
        for tc in range(4):
            for oh in range(2):
                bki = (2 * tc + oh) % 2
                def em_proj(tc=tc, oh=oh, bki=bki, atg=atg):
                    r = None
                    for fc in range(8):
                        r = t.matmul(
                            bank(bki),
                            atg[:, TGA * fc + 128 * tc : TGA * fc + 128 * tc + 128],
                            wpT[:, 1024 * fc + 512 * oh : 1024 * fc + 512 * oh + 512],
                            start=(fc == 0),
                            stop=False,
                        )
                    return t.matmul(
                        bank(bki),
                        rowbf[0:1, 3072 : 3072 + 128],
                        rowbf[0:1, 3584 + 512 * oh : 3584 + 512 * oh + 512],
                        start=False,
                        stop=True,
                    )
                mm = sch.add(
                    "pe",
                    em_proj,
                    (a_ld, wp_evs[7], proj_ev_last[bki]),
                    f"proj{g}.{tc}.{oh}",
                )
                def em_y2(tc=tc, oh=oh, bki=bki, xtg=xtg, sb_=sb_, y2=y2):
                    return v.scalar_tensor_tensor(
                        y2[:, 1024 * tc + 512 * oh : 1024 * tc + 512 * oh + 512],
                        bank(bki),
                        1.0,
                        xtg[:, 1024 * tc + 512 * oh : 1024 * tc + 512 * oh + 512],
                        ALU.mult,
                        ALU.add,
                        accum_out=stats[:, sb_ + 2 * tc + oh : sb_ + 2 * tc + oh + 1],
                    )
                ev = sch.add(
                    "dve", em_y2,
                    (mm, x_ld_a if tc < 2 else x_ld_b) + y2_readers_par[g % 2]
                    + ((lastA["dve"], attnC_store[0]) if g < 2 else ()),
                    f"y2{g}.{tc}.{oh}",
                )
                proj_ev_last[bki] = ev
                y2_evs.append(ev)
        attn_user[g % 2] = y2_evs[-1]
        xld_user[0] = y2_evs[3]
        xld_user[1] = y2_evs[-1]
        if b_pending[0] is not None:
            b_pending[0]()
            b_pending[0] = None
        stat_ops = []
        sq_ops = []
        for tc in range(4):
            sq = sch.add(
                "act",
                lambda tc=tc, sb_=sb_, y2=y2: sc.activation(
                    sq_scrB[:, 0:1024],
                    y2[:, 1024 * tc : 1024 * tc + 1024],
                    AF.Square,
                    accum_out=stats[:, sb_ + 12 + tc : sb_ + 13 + tc],
                ),
                (y2_evs[2 * tc], y2_evs[2 * tc + 1], prev_sq) + tuple(b_x2t_trs[0])
                + ((lastA["act"],) if g < 2 else ()),
                f"sq{g}.{tc}",
            )
            prev_sq = sq
            sq_ops.append(sq)
            s_ = sch.add(
                "dve",
                lambda tc=tc, sb_=sb_: v.scalar_tensor_tensor(
                    stats[:, sb_ + 8 + tc : sb_ + 9 + tc],
                    stats[:, sb_ + 2 * tc : sb_ + 2 * tc + 1],
                    1.0,
                    stats[:, sb_ + 2 * tc + 1 : sb_ + 2 * tc + 2],
                    ALU.mult,
                    ALU.add,
                ),
                (y2_evs[2 * tc], y2_evs[2 * tc + 1]),
                f"s{g}.{tc}",
            )
            m_ = sch.add(
                "dve",
                lambda tc=tc, sb_=sb_: v.tensor_scalar_mul(
                    stats[:, sb_ + 16 + tc : sb_ + 17 + tc],
                    stats[:, sb_ + 8 + tc : sb_ + 9 + tc],
                    1.0 / D,
                ),
                (s_,),
                f"m{g}.{tc}",
            )
            mm2_ = sch.add(
                "dve",
                lambda tc=tc, sb_=sb_: v.scalar_tensor_tensor(
                    stats[:, sb_ + 20 + tc : sb_ + 21 + tc],
                    stats[:, sb_ + 16 + tc : sb_ + 17 + tc],
                    -1.0,
                    stats[:, sb_ + 16 + tc : sb_ + 17 + tc],
                    ALU.mult,
                    ALU.mult,
                ),
                (m_,),
                f"mm{g}.{tc}",
            )
            var2 = sch.add(
                "dve",
                lambda tc=tc, sb_=sb_: v.scalar_tensor_tensor(
                    stats[:, sb_ + 24 + tc : sb_ + 25 + tc],
                    stats[:, sb_ + 12 + tc : sb_ + 13 + tc],
                    1.0 / D,
                    stats[:, sb_ + 20 + tc : sb_ + 21 + tc],
                    ALU.mult,
                    ALU.add,
                ),
                (mm2_, sq),
                f"var{g}.{tc}",
            )
            stat_ops.append(var2)
        ln_ = sch.add(
            "act",
            lambda sb_=sb_: sc.activation(
                stats[:, sb_ + 28 : sb_ + 32], stats[:, sb_ + 24 : sb_ + 28],
                AF.Ln, bias=EPS,
            ),
            tuple(stat_ops),
            f"ln{g}",
        )
        rstd_ = sch.add(
            "act",
            lambda sb_=sb_: sc.activation(
                stats[:, sb_ + 32 : sb_ + 36], stats[:, sb_ + 28 : sb_ + 32],
                AF.Exp, scale=-0.5,
            ),
            (ln_,),
            f"rstd{g}",
        )
        nmr_ = sch.add(
            "dve",
            lambda sb_=sb_: v.scalar_tensor_tensor(
                stats[:, sb_ + 36 : sb_ + 40], stats[:, sb_ + 16 : sb_ + 20],
                -1.0, stats[:, sb_ + 32 : sb_ + 36],
                ALU.mult, ALU.mult,
            ),
            (rstd_,),
            f"nmr{g}",
        )
        x2_ops = []
        x2b_ops = []
        for tc in range(4):
            ap1 = sch.add(
                "act",
                lambda tc=tc, sb_=sb_, y2=y2: sc.activation(
                    x2f[:, 1024 * tc : 1024 * tc + 1024],
                    y2[:, 1024 * tc : 1024 * tc + 1024],
                    AF.Identity,
                    bias=stats[:, sb_ + 36 + tc : sb_ + 37 + tc],
                    scale=stats[:, sb_ + 32 + tc : sb_ + 33 + tc],
                ),
                (nmr_, prev_st2, sq_ops[-1]),
                f"ap1{g}.{tc}",
            )
            ap2 = sch.add(
                "act",
                lambda tc=tc, sb_=sb_, y2=y2: sc.activation(
                    x2b[:, 1024 * tc : 1024 * tc + 1024],
                    y2[:, 1024 * tc : 1024 * tc + 1024],
                    AF.Identity,
                    bias=stats[:, sb_ + 36 + tc : sb_ + 37 + tc],
                    scale=stats[:, sb_ + 32 + tc : sb_ + 33 + tc],
                ),
                (nmr_, sq_ops[-1]) + tuple(b_x2t_trs[0]),
                f"ap2{g}.{tc}",
            )
            x2_ops.append(ap1)
            x2b_ops.append(ap2)
        y2_readers_par[g % 2] = tuple(sq_ops + x2_ops + x2b_ops)
        st2 = gp_dma(
            "ps2",
            [(
                x2_d[TGA * g : TGA * g + TGA, :].rearrange("(c p) f -> p c f", p=128),
                x2f.rearrange("p (c f) -> p c f", c=4, f=1024),
            )],
            tuple(x2_ops),
            f"x2st{g}",
        )
        prev_st2 = st2

        def fin_b(g=g, x2b_ops=x2b_ops):
            trs = []
            evs = []
            for tc in range(4):
                bki = 2 + tc % 2
                def em_tr(tc=tc, bki=bki):
                    r = None
                    for fc in range(8):
                        r = t.transpose(
                            bank_bf(bki)[:, 128 * fc : 128 * fc + 128],
                            x2b[:, 1024 * tc + 128 * fc : 1024 * tc + 128 * fc + 128],
                            idb,
                        )
                    return r
                dep_bank = x2t_ev_last[tc % 2]
                if dep_bank is None:
                    dep_bank = wp_evs[-1]
                tr = sch.add("pe", em_tr, (x2b_ops[tc], dep_bank), f"x2t{g}.{tc}")
                def em_ev(tc=tc, bki=bki):
                    return v.tensor_copy(
                        x2T4[:, :, tc, :],
                        bank_bf(bki).rearrange("p (a c) -> p a c", a=8, c=128),
                    )
                ev = sch.add("dve", em_ev, (tr, b_st1[0]), f"x2tev{g}.{tc}")
                x2t_ev_last[tc % 2] = ev
                trs.append(tr)
                evs.append(ev)
            b_x2t_trs[0] = trs
            st1 = gp_dma(
                "ps1",
                [
                    (x2T_d[fc][:, TGA * g : TGA * g + TGA], x2Ttg[:, TGA * fc : TGA * fc + TGA])
                    for fc in range(8)
                ],
                tuple(evs),
                f"x2Tst{g}",
            )
            b_st1[0] = st1

        b_pending[0] = fin_b

    if b_pending[0] is not None:
        b_pending[0]()
        b_pending[0] = None

    sch.barrier()

    w2sb = bf(C_w2, 32 * 1024)
    sq_scrC = f32v(C_scr, 1024)
    w2_lds = []
    for hc in range(32):
        ld = gp_dma(
            f"pw{hc % 2}",
            [(w2sb[:, 1024 * hc : 1024 * hc + 1024], w2_d[128 * hc : 128 * hc + 128, :])],
            (w2_lds[hc - 2] if hc >= 2 else None,),
            f"w2ld{hc}",
        )
        w2_lds.append(ld)

    F1BANKS = [0, 1, 6, 7]
    h_user = [None, None, None, None]
    gelu_bank = [None, None, None, None]
    f1ops = {}
    x2t_user = [None, None]
    x2_user = [(), ()]
    prev_tap = [None]
    out_store = [None, None]
    ffn2_bank_user = [None, None, None, None]
    prev_sqC = None
    ld1s = [None] * (NTGC + 1)
    ld2s = [None] * (NTGC + 1)

    def c_loads(g):
        if g >= NTGC:
            return
        gb = g % 2
        x2Tt = bf(C_x2T + gb * 8 * TGC, 8 * TGC)
        x2t = f32v(C_x2 + gb * 4096, 2048)
        ld1s[g] = sp_dma(
            f"sa{gb}",
            [
                (x2Tt[:, TGC * fc : TGC * fc + TGC], x2T_d[fc][:, TGC * g : TGC * g + TGC])
                for fc in range(8)
            ],
            (x2t_user[gb],),
            f"cx2T{g}",
        )
        ld2s[g] = sp_dma(
            f"sx{gb}",
            [(
                x2t.rearrange("p (c f) -> p c f", c=2, f=1024),
                x2_d[TGC * g : TGC * g + TGC, :].rearrange("(c p) f -> p c f", p=128),
            )],
            x2_user[gb],
            f"cx2{g}",
        )

    def mk_f1c(ui):
        g, hc = ui // 32, ui % 32
        gb = g % 2
        x2Tt = bf(C_x2T + gb * 8 * TGC, 8 * TGC)
        hb = ui % 4
        bki = F1BANKS[hb]
        def em_f1(hc=hc, bki=bki, x2Tt=x2Tt):
            r = None
            for fc in range(8):
                r = t.matmul(
                    bank(bki)[:, 0:TGC],
                    w1sb[:, FF * fc + 128 * hc : FF * fc + 128 * hc + 128],
                    x2Tt[:, TGC * fc : TGC * fc + TGC],
                    start=(fc == 0),
                    stop=(fc == 7),
                )
            return r
        deps1 = [ld1s[g], gelu_bank[hb]]
        if ui == 0:
            deps1 += w1_lds
        op = sch.add("pe", em_f1, tuple(deps1), f"f1_{g}.{hc}")
        f1ops[ui] = op
        return op

    def mk_gelu(ui):
        g, hc = ui // 32, ui % 32
        hb = ui % 4
        htile = bf(C_h + hb * TGC, TGC)
        gl = sch.add(
            "act",
            lambda hc=hc, hb=hb, htile=htile: sc.activation(
                htile, bank(F1BANKS[hb])[:, 0:TGC], AF.Gelu, bias=b1f[:, hc : hc + 1]
            ),
            (f1ops[ui], h_user[hb]) + ((eb1,) if ui == 0 else ()),
            f"gelu{g}.{hc}",
        )
        gelu_bank[hb] = gl
        return gl

    c_loads(0)
    c_loads(1)
    f1q = [mk_f1c(0), mk_f1c(1), mk_f1c(2), mk_f1c(3)]
    glq = [mk_gelu(0), mk_gelu(1), mk_gelu(2), mk_gelu(3)]
    NUC = NTGC * 32
    for ui in range(NUC):
        g, hc = ui // 32, ui % 32
        gb = g % 2
        x2t = f32v(C_x2 + gb * 4096, 2048)
        outt = f32v(C_out + gb * 4096, 2048)
        sb_ = 48 * gb
        hb = ui % 4
        htile = bf(C_h + hb * TGC, TGC)
        mm1 = f1q.pop(0)
        gl = glq.pop(0)
        if ui + 4 < NUC:
            f1q.append(mk_f1c(ui + 4))
        def em_f2(hc=hc, htile=htile, first=(hc == 0), last=(hc == 31)):
            r = None
            for tc in range(2):
                for oh in range(2):
                    r = t.matmul(
                        bank(2 + 2 * tc + oh),
                        htile[:, 128 * tc : 128 * tc + 128],
                        w2sb[:, 1024 * hc + 512 * oh : 1024 * hc + 512 * oh + 512],
                        start=first,
                        stop=False,
                    )
            if last:
                for tc in range(2):
                    for oh in range(2):
                        r = t.matmul(
                            bank(2 + 2 * tc + oh),
                            rowbf[0:1, 3072 : 3072 + 128],
                            rowbf[0:1, 4608 + 512 * oh : 4608 + 512 * oh + 512],
                            start=False,
                            stop=True,
                        )
            return r
        deps_f2 = [gl, w2_lds[hc]]
        if hc == 0:
            deps_f2 += [d for d in ffn2_bank_user if d is not None]
        mm2 = sch.add("pe", em_f2, tuple(deps_f2), f"f2_{g}.{hc}")
        h_user[hb] = mm2
        if ui + 4 < NUC:
            glq.append(mk_gelu(ui + 4))
        if hc != 31:
            continue
        x2t_user[gb] = mm1
        ev_ops = []
        for tc in range(2):
            for oh in range(2):
                def em_y3(tc=tc, oh=oh, x2t=x2t, sb_=sb_):
                    return v.scalar_tensor_tensor(
                        x2t[:, 1024 * tc + 512 * oh : 1024 * tc + 512 * oh + 512],
                        bank(2 + 2 * tc + oh),
                        1.0,
                        x2t[:, 1024 * tc + 512 * oh : 1024 * tc + 512 * oh + 512],
                        ALU.mult,
                        ALU.add,
                        accum_out=stats[:, sb_ + 2 * tc + oh : sb_ + 2 * tc + oh + 1],
                    )
                ev = sch.add("dve", em_y3, (mm2, ld2s[g]), f"y3{g}.{tc}.{oh}")
                ffn2_bank_user[2 * tc + oh] = ev
                ev_ops.append(ev)
        stat_ops = []
        sq_ops = []
        for tc in range(2):
            sq = sch.add(
                "act",
                lambda tc=tc, x2t=x2t, sb_=sb_: sc.activation(
                    sq_scrC[:, 0:1024],
                    x2t[:, 1024 * tc : 1024 * tc + 1024],
                    AF.Square,
                    accum_out=stats[:, sb_ + 12 + tc : sb_ + 13 + tc],
                ),
                (ev_ops[2 * tc], ev_ops[2 * tc + 1], prev_sqC),
                f"csq{g}.{tc}",
            )
            prev_sqC = sq
            sq_ops.append(sq)
            s_ = sch.add(
                "dve",
                lambda tc=tc, sb_=sb_: v.scalar_tensor_tensor(
                    stats[:, sb_ + 8 + tc : sb_ + 9 + tc],
                    stats[:, sb_ + 2 * tc : sb_ + 2 * tc + 1],
                    1.0,
                    stats[:, sb_ + 2 * tc + 1 : sb_ + 2 * tc + 2],
                    ALU.mult,
                    ALU.add,
                ),
                (ev_ops[2 * tc], ev_ops[2 * tc + 1]),
                f"cs{g}.{tc}",
            )
            m_ = sch.add(
                "dve",
                lambda tc=tc, sb_=sb_: v.tensor_scalar_mul(
                    stats[:, sb_ + 16 + tc : sb_ + 17 + tc],
                    stats[:, sb_ + 8 + tc : sb_ + 9 + tc],
                    1.0 / D,
                ),
                (s_,),
                f"cm{g}.{tc}",
            )
            mm2_ = sch.add(
                "dve",
                lambda tc=tc, sb_=sb_: v.scalar_tensor_tensor(
                    stats[:, sb_ + 20 + tc : sb_ + 21 + tc],
                    stats[:, sb_ + 16 + tc : sb_ + 17 + tc],
                    -1.0,
                    stats[:, sb_ + 16 + tc : sb_ + 17 + tc],
                    ALU.mult,
                    ALU.mult,
                ),
                (m_,),
                f"cmm{g}.{tc}",
            )
            var2 = sch.add(
                "dve",
                lambda tc=tc, sb_=sb_: v.scalar_tensor_tensor(
                    stats[:, sb_ + 24 + tc : sb_ + 25 + tc],
                    stats[:, sb_ + 12 + tc : sb_ + 13 + tc],
                    1.0 / D,
                    stats[:, sb_ + 20 + tc : sb_ + 21 + tc],
                    ALU.mult,
                    ALU.add,
                ),
                (mm2_, sq),
                f"cvar{g}.{tc}",
            )
            stat_ops.append(var2)
        ln_ = sch.add(
            "act",
            lambda sb_=sb_: sc.activation(
                stats[:, sb_ + 28 : sb_ + 30], stats[:, sb_ + 24 : sb_ + 26],
                AF.Ln, bias=EPS,
            ),
            tuple(stat_ops),
            f"cln{g}",
        )
        rstd_ = sch.add(
            "act",
            lambda sb_=sb_: sc.activation(
                stats[:, sb_ + 32 : sb_ + 34], stats[:, sb_ + 28 : sb_ + 30],
                AF.Exp, scale=-0.5,
            ),
            (ln_,),
            f"crstd{g}",
        )
        nmr_ = sch.add(
            "dve",
            lambda sb_=sb_: v.scalar_tensor_tensor(
                stats[:, sb_ + 36 : sb_ + 38], stats[:, sb_ + 16 : sb_ + 18],
                -1.0, stats[:, sb_ + 32 : sb_ + 34],
                ALU.mult, ALU.mult,
            ),
            (rstd_,),
            f"cnmr{g}",
        )
        ap_ops = []
        for tc in range(2):
            ap1 = sch.add(
                "act",
                lambda tc=tc, x2t=x2t, outt=outt, sb_=sb_: sc.activation(
                    outt[:, 1024 * tc : 1024 * tc + 1024],
                    x2t[:, 1024 * tc : 1024 * tc + 1024],
                    AF.Identity,
                    bias=stats[:, sb_ + 36 + tc : sb_ + 37 + tc],
                    scale=stats[:, sb_ + 32 + tc : sb_ + 33 + tc],
                ),
                (nmr_, out_store[gb], sq_ops[-1]),
                f"cap{g}.{tc}",
            )
            ap_ops.append(ap1)
        x2_user[gb] = tuple(ev_ops + sq_ops + ap_ops)
        if debug_taps:
            tapop = gp_dma(
                "pbias",
                [(
                    y3_tap[TGC * g : TGC * g + TGC, :].rearrange("(c p) f -> p c f", p=128),
                    x2t.rearrange("p (c f) -> p c f", c=2, f=1024),
                )],
                tuple(ev_ops) + ((prev_tap[0],) if prev_tap[0] is not None else ()),
                f"y3tap{g}",
            )
            prev_tap[0] = tapop
            x2_user[gb] = x2_user[gb] + (tapop,)
        c_loads(g + 2)
        st = gp_dma(
            f"pa{g % 2}",
            [(
                out_d[TGC * g : TGC * g + TGC, :].rearrange("(c p) f -> p c f", p=128),
                outt.rearrange("p (c f) -> p c f", c=2, f=1024),
            )],
            tuple(ap_ops),
            f"outst{g}",
        )
        out_store[gb] = st

    sch.finalize()
    with nc.Block() as blk:
        @blk.tensor
        def _(eng):
            sch.emit_engine("pe", eng)

        @blk.scalar
        def _(eng):
            sch.emit_engine("act", eng)

        @blk.vector
        def _(eng):
            sch.emit_engine("dve", eng)

        @blk.gpsimd
        def _(eng):
            sch.emit_engine("pool", eng)

        @blk.sync
        def _(eng):
            sch.emit_engine("sp", eng)

    return nc


def make_in_maps(inputs, n_cores=8):
    x = np.ascontiguousarray(inputs["L0_attn_hidden_states"])
    shared = {
        "wqkv": np.ascontiguousarray(inputs["L0_attn_c_attn_kernel"]),
        "bqkv": np.ascontiguousarray(inputs["L0_attn_c_attn_bias"]),
        "wp": np.ascontiguousarray(inputs["L0_attn_c_proj_kernel"]),
        "bp": np.ascontiguousarray(inputs["L0_attn_c_proj_bias"]),
        "w1": np.ascontiguousarray(inputs["L0_ffn_w1"]),
        "b1": np.ascontiguousarray(inputs["L0_ffn_b1"]),
        "w2": np.ascontiguousarray(inputs["L0_ffn_w2"]),
        "b2": np.ascontiguousarray(inputs["L0_ffn_b2"]),
    }
    return [dict(shared, x=np.ascontiguousarray(x[i])) for i in range(n_cores)]


def kernel(**inputs):
    n_cores = 8
    nc = build_nc(S=4096)
    in_maps = make_in_maps(inputs, n_cores)
    res = run_bass_kernel_spmd(nc, in_maps, core_ids=list(range(n_cores)))
    return np.stack([r["out"] for r in res.results], axis=0)
